# version 07
# baseline (speedup 1.0000x reference)
import os as _os
import numpy as np
from contextlib import ExitStack

import concourse.bass as bass
import concourse.mybir as mybir
from concourse.bass_utils import run_bass_kernel_spmd

F32 = mybir.dt.float32
BF16 = mybir.dt.bfloat16
AF = mybir.ActivationFunctionType
ALU = mybir.AluOpType
AX = mybir.AxisListType

MAGIC = 1.5 * 2.0**23
NCOR = 8
T = 1024
D = 2048
KC = 16
H = 8192
HC = 64
OC = 16
MS = 8
NW = float(H * D)
PIECE = 128 * 2048
SLICE_ELEMS = 24 * PIECE
GATE_OFF = 0
VAL_OFF = 8 * PIECE
OUT_OFF = 16 * PIECE
CC1_N = 4160
HL = 512

_SUBLAYERS = 4
_PHASES = int(_os.environ.get("KPHASES", "99"))


class Ctr:
    __slots__ = ("s", "v")

    def __init__(self, s):
        self.s = s
        self.v = 0


def build_nc():
    nc = bass.Bass("TRN2", target_bir_lowering=False, debug=False)

    def inp(name, shape, dt=F32):
        return nc.dram_tensor(name, list(shape), dt, kind="ExternalInput").ap()

    xT = inp("xT", [D, T])
    igw = inp("igw", [KC, 128, 128])
    agw = inp("agw", [KC, 128, 128])
    igb = inp("igb", [128, KC])
    agb = inp("agb", [128, KC])
    apar = inp("apar", [128, KC])
    dvec = inp("dvec", [128, KC])
    gbias = inp("gbias", [128, HC])
    vbias = inp("vbias", [128, HC])
    obias = inp("obias", [128, OC])
    wsg = inp("wsg", [MS // 2, 128, 4096])
    wsv = inp("wsv", [MS // 2, 128, 4096])
    wso = inp("wso", [MS // 2, 128, 4096])
    wbr = inp("wbr", [D, 64])
    wbsT = inp("wbsT", [2, 128, 64])
    wcr = inp("wcr", [2, 128, 64])
    wcm = inp("wcm", [64, D])
    kg = inp("kg", [8, 2, 128, 256])
    km = inp("km", [8, 2, 128, 256])
    eye = inp("eye", [128, 128])
    onescol = inp("onescol", [128, 1])
    ones_r = inp("ones_r", [1, 128])
    rmv = inp("rmv", [128, 2])
    mkv = inp("mkv", [128, 8])
    sel8 = inp("sel8", [8, 1024])

    yT = nc.dram_tensor("yT", [D, T], F32, kind="ExternalOutput").ap()

    cc1_in = nc.dram_tensor("cc1_in", [1, CC1_N], F32).ap()
    cc1_out = nc.dram_tensor("cc1_out", [NCOR, CC1_N], F32,
                             addr_space="Shared").ap()
    tsA = [nc.dram_tensor(f"tsA{K}", [1, 4 * PIECE], BF16).ap()
           for K in range(4)]
    talA = [nc.dram_tensor(f"talA{K}", [NCOR, 4 * PIECE], BF16,
                           addr_space="Shared").ap() for K in range(4)]
    tsliceB = nc.dram_tensor("tsliceB", [1, 8 * PIECE], BF16).ap()
    cc0_in = nc.dram_tensor("cc0_in", [1, 64], F32).ap()
    cc0_out = nc.dram_tensor("cc0_out", [NCOR, 64], F32,
                             addr_space="Shared").ap()
    tallB = nc.dram_tensor("tallB", [NCOR, 8 * PIECE], BF16,
                           addr_space="Shared").ap()
    yb0p = nc.dram_tensor("yb0p", [KC // 2, 128, 2 * T], F32).ap()
    yb1p = nc.dram_tensor("yb1p", [KC // 2, 128, 2 * T], F32).ap()

    xT_c = xT.rearrange("(kc p) t -> kc p t", p=128)
    yT_c = yT.rearrange("(kc p) t -> kc p t", p=128)

    def half(buf, o):
        return buf[o // 2, :, (o % 2) * T:(o % 2) * T + T]

    with ExitStack() as ctx:
        AT = lambda n, s, d, off: nc.alloc_sbuf_tensor_at(n, s, d, offset=off)

        wrot = [AT("wrot0", [128, KC, 128], BF16, 52224),
                AT("wrot1", [128, KC, 128], BF16, 56320),
                AT("wrot2", [128, KC, 128], BF16, 60416)]
        wbr_b = AT("wbr_b", [128, KC, 64], BF16, 52224)
        wcm_b = AT("wcm_b", [64, D], BF16, 56320)
        tstg = [AT("tstg0", [128, 2048], BF16, 52224),
                AT("tstg1", [128, 2048], BF16, 56320)]

        actq = AT("actq", [128, KC, T], BF16, 64512)
        wstage = [AT("wstage0", [128, 2048], F32, 64512),
                  AT("wstage1", [128, 2048], F32, 72704),
                  AT("wstage2", [128, 2048], F32, 80896)]
        igw_sb = AT("igw_sb", [128, KC, 128], F32, 80896)
        agw_sb = AT("agw_sb", [128, KC, 128], F32, 89088)
        exA = AT("exA", [128, NCOR, KC], F32, 60416)
        exH = AT("exH", [128, NCOR, KC], F32, 60928)
        hin = AT("hin", [128, KC], F32, 61440)
        hcore = AT("hcore", [128, KC], F32, 61504)
        lsum = AT("lsum", [128, KC], F32, 61568)
        acore = AT("acore", [128, KC], F32, 61632)
        exg = AT("exg", [128, KC], F32, 61696)
        st8 = AT("st8", [8, 16], F32, 61760)
        wst3 = AT("wst3", [128, 4], F32, 61824)

        abuf = AT("abuf", [128, KC, T], F32, 97280)
        wpair = [AT("wpair0", [128, 4096], F32, 113664),
                 AT("wpair1", [128, 4096], F32, 130048)]
        xtb = AT("xtb", [128, KC, T], F32, 162816)
        hmid = AT("hmid", [128, HC, T], BF16, 97280)
        xmm = AT("xmm", [128, KC, T], F32, 97280)
        ymm = AT("ymm", [128, KC, T], F32, 101376)
        k_u = AT("k_u", [128, 2, 256], F32, 97280)
        k_m = AT("k_m", [128, 2, 256], F32, 99328)
        k_l = AT("k_l", [128, 2, 256], F32, 101376)
        k_r = [AT("k_r0", [128, 2, 256], F32, 103424),
               AT("k_r1", [128, 2, 256], F32, 105472)]
        k_s1 = AT("k_s1", [64, 256], F32, 107520)
        k_s1T = AT("k_s1T", [128, 2, 64], F32, 108544)
        t1b = AT("t1b", [64, T], BF16, 97280)
        t2b = AT("t2b", [64, T], BF16, 99328)
        wbsT_sb = AT("wbsT_sb", [128, 2, 64], F32, 109568)
        wcr_sb = AT("wcr_sb", [128, 2, 64], F32, 110080)
        wmid_f = AT("wmid_f", [64, 64], F32, 110592)

        sb = lambda n, s, d: ctx.enter_context(nc.sbuf_tensor(n, s, d))
        tA = sb("tA", [128, T], F32)
        tB = sb("tB", [128, T], F32)
        tC = sb("tC", [128, T], F32)
        bc0 = sb("bc0", [128, T], F32)
        bc1 = sb("bc1", [128, T], F32)
        bc2 = sb("bc2", [128, T], F32)
        mxa = sb("mxa", [128, T], BF16)
        mna = sb("mna", [128, T], BF16)
        gb_sb = sb("gb_sb", [128, HC], F32)
        vb_sb = sb("vb_sb", [128, HC], F32)
        ob_sb = sb("ob_sb", [128, OC], F32)
        igb_sb = sb("igb_sb", [128, KC], F32)
        agb_sb = sb("agb_sb", [128, KC], F32)
        psp = sb("psp", [128, KC], F32)
        dv_sb = sb("dv_sb", [128, KC], F32)
        eye_sb = sb("eye_sb", [128, 128], F32)
        onesc_sb = sb("onesc_sb", [128, 1], F32)
        onesr_sb = sb("onesr_sb", [1, 128], F32)
        rmv_sb = sb("rmv_sb", [128, 2], F32)
        mkv_sb = sb("mkv_sb", [128, 8], F32)
        gam6 = sb("gam6", [128, 8], F32)
        wmid_b = sb("wmid_b", [64, 64], BF16)
        sel8_sb = sb("sel8_sb", [8, 1024], F32)

        assert nc.sbuf_base <= 52224, f"auto SBUF overflow: base={nc.sbuf_base}"

        PA = ctx.enter_context(nc.psum_tensor("PA", [128, T], F32))
        PB = ctx.enter_context(nc.psum_tensor("PB", [128, T], F32))
        PC = ctx.enter_context(nc.psum_tensor("PC", [128, T], F32))
        PD = ctx.enter_context(nc.psum_tensor("PD", [128, T], F32))
        PGV = [PA, PB, PC, PD]

        sem = lambda n: ctx.enter_context(nc.semaphore(n))
        dC = Ctr(sem("dC"))
        dXs = [Ctr(sem("dX0")), Ctr(sem("dX1"))]
        dX4 = dXs + [Ctr(sem("dX2")), Ctr(sem("dX3"))]
        dWs = [Ctr(sem("dW0")), Ctr(sem("dW1")), Ctr(sem("dW2"))]
        dGu = Ctr(sem("dGu"))
        dGm = Ctr(sem("dGm"))
        dOs = [Ctr(sem("dO0")), Ctr(sem("dO1"))]
        dOQ = Ctr(sem("dOQ"))
        dTS2 = [Ctr(sem("dTSa")), Ctr(sem("dTSb"))]
        dCI = Ctr(sem("dCI"))
        dU1 = Ctr(sem("dU1"))
        dU2 = Ctr(sem("dU2"))
        dF = Ctr(sem("dF"))
        cT = Ctr(sem("cT"))
        cV = Ctr(sem("cV"))
        cS = Ctr(sem("cS"))
        cG = Ctr(sem("cG"))
        ccc = Ctr(sem("ccc"))

        block = ctx.enter_context(nc.Block())

        def vop(h, name, *a, **kw):
            h.wait_ge(cV.s, cV.v)
            getattr(h, name)(*a, **kw).then_inc(cV.s, 1)
            cV.v += 1
            return cV.v

        def sop(h, name, *a, **kw):
            h.wait_ge(cS.s, cS.v)
            getattr(h, name)(*a, **kw).then_inc(cS.s, 1)
            cS.v += 1
            return cS.v

        def gop(h, name, *a, **kw):
            h.wait_ge(cG.s, cG.v)
            getattr(h, name)(*a, **kw).then_inc(cG.s, 1)
            cG.v += 1
            return cG.v

        def dma(h, ctr, out, in_, **kw):
            h.dma_start(out=out, in_=in_, **kw).then_inc(ctr.s, 16)
            ctr.v += 16
            return ctr.v

        def tmm(h, mms):
            for i, (o, l, r, st, sp) in enumerate(mms):
                ins = h.matmul(o, l, r, start=st, stop=sp)
                if i == len(mms) - 1:
                    ins.then_inc(cT.s, 1)
                    cT.v += 1
            return cT.v

        def ttr(h, out, in_, ident):
            h.transpose(out, in_, ident).then_inc(cT.s, 1)
            cT.v += 1
            return cT.v

        vsec = block.vector
        ssec = block.scalar
        tsec = block.tensor
        gsec = block.gpsimd
        ysec = block.sync

        def halves(P_, lhsT, rhs_fn, st, sp):
            return [(P_[:, hh * HL:(hh + 1) * HL], lhsT, rhs_fn(hh), st, sp)
                    for hh in range(2)]

        consts = [
            (igb_sb[:, :], igb), (agb_sb[:, :], agb), (psp[:, :], apar),
            (dv_sb[:, :], dvec), (gb_sb[:, :], gbias), (vb_sb[:, :], vbias),
            (ob_sb[:, :], obias), (eye_sb[:, :], eye), (onesc_sb[:, :], onescol),
            (onesr_sb[:, :], ones_r), (rmv_sb[:, :], rmv), (mkv_sb[:, :], mkv),
            (sel8_sb[:, :], sel8),
            (wbsT_sb[:, :, :], wbsT.rearrange("b p r -> p b r")),
            (wcr_sb[:, :, :], wcr.rearrange("b p r -> p b r")),
        ]

        @ysec
        def _(sync):
            for dst, src in consts:
                dma(sync, dC, dst, src)
        t_cst = dC.v

        @ssec
        def _(scalar):
            scalar.wait_ge(dC.s, t_cst)
            sop(scalar, "activation", psp[:, :], psp[:, :], AF.Exp)
            sop(scalar, "activation", psp[:, :], psp[:, :], AF.Ln, bias=1.0)
        s_psp0 = cS.v

        @vsec
        def _(vector):
            vector.wait_ge(cS.s, s_psp0)
            vop(vector, "tensor_scalar", psp[:, :], psp[:, :], -8.0, None,
                ALU.mult, ALU.bypass)
        v_psp = cV.v

        kg_r = kg.rearrange("l b p c -> l p b c")
        km_r = km.rearrange("l b p c -> l p b c")
        kdma_t = []
        kv_t = []
        kt_t = []
        for i in range(8):
            @gsec
            def _(gpsimd, i=i):
                if i >= 1:
                    gpsimd.wait_ge(cV.s, kv_t[i - 1][0])
                dma(gpsimd, dGu, k_u[:, :, :], kg_r[i])
                dma(gpsimd, dGm, k_m[:, :, :], km_r[i])
            kdma_t.append((dGu.v, dGm.v))

            @vsec
            def _(vector, i=i):
                vector.wait_ge(dGu.s, kdma_t[i][0])
                vector.wait_ge(dGm.s, kdma_t[i][1])
                if i >= 2:
                    vector.wait_ge(cT.s, kt_t[i - 1])
                dst = k_r[0] if i == 0 else k_l
                lb = vop(vector, "tensor_tensor", dst[:, :, :], k_u[:, :, :],
                         k_m[:, :, :], ALU.mult)
                kv_t.append([lb])

            if i >= 1:
                @tsec
                def _(tensor, i=i):
                    tensor.wait_ge(cV.s, kv_t[i][0])
                    src = k_r[(i - 1) % 2]
                    tmm(tensor, [
                        ([PA, PB][mblk][:, 0:256],
                         k_l[:, kblk, mblk * 128:(mblk + 1) * 128],
                         src[:, kblk, :], kblk == 0, kblk == 1)
                        for mblk in range(2) for kblk in range(2)])
                kt_t.append(cT.v)

                @vsec
                def _(vector, i=i):
                    vector.wait_ge(cT.s, kt_t[i])
                    out = k_r[i % 2]
                    vop(vector, "tensor_copy", out[:, 0, :], PA[:, 0:256])
                    ev = vop(vector, "tensor_copy", out[:, 1, :], PB[:, 0:256])
                    kv_t[i].append(ev)
            else:
                kt_t.append(None)

        @tsec
        def _(tensor):
            tensor.wait_ge(cV.s, kv_t[7][1])
            tensor.wait_ge(dC.s, t_cst)
            tmm(tensor, [(PA[0:64, 0:256], wbsT_sb[:, kb, :], k_r[1][:, kb, :],
                          kb == 0, kb == 1) for kb in range(2)])
        t_s1 = cT.v

        @vsec
        def _(vector):
            vector.wait_ge(cT.s, t_s1)
            vop(vector, "tensor_copy", k_s1[:, :], PA[0:64, 0:256])
        v_s1 = cV.v

        @tsec
        def _(tensor):
            tensor.wait_ge(cV.s, v_s1)
            ttr(tensor, PB[:, 0:64], k_s1[:, 0:128], eye_sb[0:64, 0:64])
        t_tr1 = cT.v

        @vsec
        def _(vector):
            vector.wait_ge(cT.s, t_tr1)
            vop(vector, "tensor_copy", k_s1T[:, 0, :], PB[:, 0:64])
        v_tr1 = cV.v

        @tsec
        def _(tensor):
            tensor.wait_ge(cV.s, v_tr1)
            ttr(tensor, PB[:, 64:128], k_s1[:, 128:256], eye_sb[0:64, 0:64])
        t_tr2 = cT.v

        @vsec
        def _(vector):
            vector.wait_ge(cT.s, t_tr2)
            vop(vector, "tensor_copy", k_s1T[:, 1, :], PB[:, 64:128])
        v_tr2 = cV.v

        @tsec
        def _(tensor):
            tensor.wait_ge(cV.s, v_tr2)
            tmm(tensor, [(PA[0:64, 0:64], k_s1T[:, jb, :], wcr_sb[:, jb, :],
                          jb == 0, jb == 1) for jb in range(2)])
        t_wm = cT.v

        @vsec
        def _(vector):
            vector.wait_ge(cT.s, t_wm)
            vop(vector, "tensor_copy", wmid_f[:, :], PA[0:64, 0:64])
            vop(vector, "tensor_copy", wmid_b[:, :], wmid_f[:, :])
        v_wm = cV.v

        x0_d = []
        x0_v = []
        ws_d = []
        ws_v = []

        def emit_ws_piece(i):
            w, piece = divmod(i, 4)

            @ysec
            def _(sync, i=i, w=w, piece=piece):
                if i >= 2:
                    sync.wait_ge(cV.s, ws_v[i - 2])
                dma(sync, dWs[i % 2], wpair[i % 2][:, :],
                    [wsg, wsv, wso][w][piece])
            ws_d.append(dWs[i % 2].v)

            @vsec
            def _(vector, i=i, w=w):
                vector.wait_ge(dWs[i % 2].s, ws_d[i])
                if i % 4 == 0:
                    vop(vector, "reduce_sum", wst3[:, w:w + 1],
                        wpair[i % 2][:, :], AX.X, apply_absolute_value=True)
                else:
                    vop(vector, "reduce_sum", wst3[:, 3:4], wpair[i % 2][:, :],
                        AX.X, apply_absolute_value=True)
                    vop(vector, "tensor_tensor", wst3[:, w:w + 1],
                        wst3[:, w:w + 1], wst3[:, 3:4], ALU.add)
            ws_v.append(cV.v)

        for kc in range(KC):
            @ysec
            def _(sync, kc=kc):
                if kc >= 4:
                    sync.wait_ge(cV.s, x0_v[kc - 4])
                dma(sync, dX4[kc % 4], xtb[:, kc, :], xT_c[kc])
            x0_d.append(dX4[kc % 4].v)

            @vsec
            def _(vector, kc=kc):
                sq = [tA, tB][kc % 2]
                vector.wait_ge(dX4[kc % 4].s, x0_d[kc])
                vop(vector, "tensor_tensor", sq[:, :], xtb[:, kc, :],
                    xtb[:, kc, :], ALU.mult)
                if kc == 0:
                    vop(vector, "tensor_copy", PC[:, :], xtb[:, kc, :])
                    vop(vector, "tensor_copy", PD[:, :], sq[:, :])
                else:
                    vop(vector, "tensor_tensor", PC[:, :], PC[:, :],
                        xtb[:, kc, :], ALU.add)
                    vop(vector, "tensor_tensor", PD[:, :], PD[:, :], sq[:, :],
                        ALU.add)
            x0_v.append(cV.v)
            if kc % 4 != 3:
                emit_ws_piece(len(ws_d))
        v_ln1 = max(x0_v)

        if _PHASES < 1:
            @ysec
            def _(sync):
                sync.wait_ge(cV.s, max(v_wm, ws_v[-1], v_ln1))
                dma(sync, dF, yT[:, :], xT[:, :])
                sync.wait_ge(dF.s, dF.v)
            return nc

        def emit_ln_glue(waits):
            b = {}

            @vsec
            def _(vector):
                for w_ in waits:
                    vector.wait_ge(*w_)
                vop(vector, "tensor_copy", tA[:, :], PC[:, :])
                vop(vector, "tensor_copy", tB[:, :], PD[:, :])
            b["vp"] = cV.v

            @tsec
            def _(tensor):
                tensor.wait_ge(cV.s, b["vp"])
                g = []
                for hh in range(2):
                    g.append((PC[0:1, hh * HL:(hh + 1) * HL], onesc_sb[:, 0:1],
                              tA[:, hh * HL:(hh + 1) * HL], True, True))
                    g.append((PD[0:1, hh * HL:(hh + 1) * HL], onesc_sb[:, 0:1],
                              tB[:, hh * HL:(hh + 1) * HL], True, True))
                tmm(tensor, g)
            b["tp"] = cT.v

            @vsec
            def _(vector):
                vector.wait_ge(cT.s, b["tp"])
                vop(vector, "tensor_copy", tA[0:1, :], PC[0:1, :])
                vop(vector, "tensor_scalar", tA[0:1, :], tA[0:1, :], 1.0 / D,
                    None, ALU.mult, ALU.bypass)
                vop(vector, "tensor_copy", tB[0:1, :], PD[0:1, :])
                vop(vector, "tensor_scalar", tB[0:1, :], tB[0:1, :], 1.0 / D,
                    None, ALU.mult, ALU.bypass)
                vop(vector, "tensor_tensor", tC[0:1, :], tA[0:1, :], tA[0:1, :],
                    ALU.mult)
                vop(vector, "tensor_tensor", tB[0:1, :], tB[0:1, :], tC[0:1, :],
                    ALU.subtract)
                vop(vector, "tensor_scalar", tB[0:1, :], tB[0:1, :], 1e-5, None,
                    ALU.add, ALU.bypass)
            b["v0"] = cV.v

            @ssec
            def _(scalar):
                scalar.wait_ge(cV.s, b["v0"])
                sop(scalar, "activation", tB[0:1, :], tB[0:1, :], AF.Sqrt)
            b["s0"] = cS.v

            @vsec
            def _(vector):
                vector.wait_ge(cS.s, b["s0"])
                vop(vector, "reciprocal", tB[0:1, :], tB[0:1, :])
            b["v1"] = cV.v

            @tsec
            def _(tensor):
                tensor.wait_ge(cV.s, b["v1"])
                g = []
                for hh in range(2):
                    g.append((PC[:, hh * HL:(hh + 1) * HL], onesr_sb[0:1, :],
                              tA[0:1, hh * HL:(hh + 1) * HL], True, True))
                    g.append((PD[:, hh * HL:(hh + 1) * HL], onesr_sb[0:1, :],
                              tB[0:1, hh * HL:(hh + 1) * HL], True, True))
                tmm(tensor, g)
            b["t0"] = cT.v

            @vsec
            def _(vector):
                vector.wait_ge(cT.s, b["t0"])
                vop(vector, "tensor_copy", bc0[:, :], PC[:, :])
                vop(vector, "tensor_copy", bc1[:, :], PD[:, :])
            b["v2"] = cV.v
            return b

        @tsec
        def _(tensor):
            tensor.wait_ge(cV.s, ws_v[-1])
            tensor.wait_ge(cV.s, v_wm)
            tmm(tensor, [(PB[0:1, 0:4], onesc_sb[:, 0:1], wst3[:, 0:4],
                          True, True)])
        t_st = cT.v

        @vsec
        def _(vector):
            vector.wait_ge(cT.s, t_st)
            vop(vector, "tensor_copy", st8[0:1, 0:4], PB[0:1, 0:4])
        v_st = cV.v

        @gsec
        def _(gpsimd):
            gpsimd.wait_ge(cV.s, v_st)
            dma(gpsimd, dCI, cc0_in[0:1, 0:4], st8[0:1, 0:4])
            dma(gpsimd, dCI, cc0_in[0:1, 4:64], eye_sb[0:1, 0:60])
            gpsimd.wait_ge(dCI.s, dCI.v)
            gpsimd.collective_compute(
                "AllGather", ALU.bypass, replica_groups=[list(range(NCOR))],
                ins=[cc0_in[:, :]], outs=[cc0_out[:, :]]).then_inc(ccc.s, 1)
            ccc.v += 1
        t_cc0 = ccc.v

        @ysec
        def _(sync):
            sync.wait_ge(ccc.s, t_cc0)
            dma(sync, dU1, st8[:, 0:4], cc0_out[:, 0:4],
                allow_slow_non_contiguous=True)
        t_st8 = dU1.v

        @tsec
        def _(tensor):
            tensor.wait_ge(dU1.s, t_st8)
            tmm(tensor, [(PB[0:1, 4:8], onesc_sb[0:8, 0:1], st8[:, 0:4],
                          True, True)])
        t_tot = cT.v

        @vsec
        def _(vector):
            vector.wait_ge(cT.s, t_tot)
            vop(vector, "tensor_scalar", st8[0:1, 4:8], PB[0:1, 4:8], 1.0 / NW,
                1e-5, ALU.mult, ALU.max)
            vop(vector, "reciprocal", st8[0:1, 8:12], st8[0:1, 4:8])
            vop(vector, "tensor_scalar", st8[0:1, 12:16], st8[0:1, 4:8],
                1.0 / 127.0, None, ALU.mult, ALU.bypass)
        v_gam = cV.v

        @tsec
        def _(tensor):
            tensor.wait_ge(cV.s, v_gam)
            tmm(tensor, [(PB[:, 8 + j:9 + j], onesr_sb[0:1, :],
                          st8[0:1, c:c + 1], True, True)
                         for j, c in enumerate([8, 9, 10, 12, 13, 14])])
        t_rep = cT.v

        @vsec
        def _(vector):
            vector.wait_ge(cT.s, t_rep)
            vop(vector, "tensor_copy", gam6[:, 0:6], PB[:, 8:14])
        v_rep = cV.v

        g1 = emit_ln_glue(waits=[(cV.s, v_ln1), (cS.s, cS.v), (cV.s, cV.v)])

        @ysec
        def _(sync):
            sync.wait_ge(cV.s, ws_v[-1])
            dma(sync, dC, igw_sb[:, :, :], igw.rearrange("k a b -> a k b"))
            dma(sync, dC, agw_sb[:, :, :], agw.rearrange("k a b -> a k b"))
        t_gw = dC.v

        tn_d = []
        tn_v = []
        tn_o = []
        agv = {}

        def emit_tern_piece(i):
            w, piece = divmod(i, MS)

            @ysec
            def _(sync, i=i, w=w, piece=piece):
                if i < 2:
                    sync.wait_ge(cV.s, ws_v[-1])
                else:
                    sync.wait_ge(cV.s, tn_v[i - 2])
                src = [wsg, wsv, wso][w][piece // 2][
                    :, (piece % 2) * 2048:(piece % 2) * 2048 + 2048]
                dma(sync, dWs[i % 2], wstage[i % 2][:, :], src)
            tn_d.append(dWs[i % 2].v)

            @vsec
            def _(vector, i=i, w=w):
                vector.wait_ge(dWs[i % 2].s, tn_d[i])
                if i == 0:
                    vector.wait_ge(cV.s, v_rep)
                if i >= 2:
                    vector.wait_ge(dTS2[i % 2].s, tn_o[i - 2])
                vop(vector, "tensor_scalar", wstage[i % 2][:, :],
                    wstage[i % 2][:, :], gam6[:, w:w + 1], 1.49,
                    ALU.mult, ALU.min)
                vop(vector, "tensor_scalar", wstage[i % 2][:, :],
                    wstage[i % 2][:, :], -1.49, MAGIC, ALU.max, ALU.add)
                vop(vector, "tensor_scalar", tstg[i % 2][:, :],
                    wstage[i % 2][:, :], -MAGIC, None, ALU.add, ALU.bypass)
            tn_v.append(cV.v)

            @gsec
            def _(gpsimd, i=i, w=w, piece=piece):
                gpsimd.wait_ge(cV.s, tn_v[i])
                if i < 16:
                    idx = (piece % 2) * 2 + w
                    dst = tsA[piece // 2][0, idx * PIECE:(idx + 1) * PIECE]
                else:
                    dst = tsliceB[0, (i - 16) * PIECE:(i - 15) * PIECE]
                dma(gpsimd, dTS2[i % 2], dst.rearrange("(p q) -> p q", p=128),
                    tstg[i % 2][:, :])
            tn_o.append(dTS2[i % 2].v)

            if i >= 9 and i % 2 == 1 and i <= 15:
                agv["wA%d" % ((i - 9) // 2)] = (dTS2[0].v, dTS2[1].v)
            if i == 23:
                agv["fin"] = (dTS2[0].v, dTS2[1].v)

        p1_xln = []
        p1_gates = []
        p1_sig = []
        p1_exp = []
        w1v = []
        for kc in range(KC + 1):
            if kc < KC:
                @vsec
                def _(vector, kc=kc):
                    buf = [tC, bc2][kc % 2]
                    if kc == 0:
                        vector.wait_ge(cV.s, g1["v2"])
                    vop(vector, "tensor_tensor", buf[:, :], xtb[:, kc, :],
                        bc0[:, :], ALU.subtract)
                    vop(vector, "tensor_tensor", buf[:, :], buf[:, :],
                        bc1[:, :], ALU.mult)
                p1_xln.append(cV.v)

                @tsec
                def _(tensor, kc=kc):
                    buf = [tC, bc2][kc % 2]
                    tensor.wait_ge(cV.s, p1_xln[kc])
                    if kc == 0:
                        tensor.wait_ge(dC.s, t_gw)
                        tensor.wait_ge(cV.s, v_wm)
                    if kc >= 1:
                        tensor.wait_ge(cS.s, p1_sig[kc - 1])
                    tmm(tensor, halves(PA, igw_sb[:, kc, :],
                                       lambda hh: buf[:, hh * HL:(hh + 1) * HL],
                                       True, True)
                        + halves(PB, agw_sb[:, kc, :],
                                 lambda hh: buf[:, hh * HL:(hh + 1) * HL],
                                 True, True))
                p1_gates.append(cT.v)

                @ssec
                def _(scalar, kc=kc):
                    sg = [tA, tB][kc % 2]
                    scalar.wait_ge(cT.s, p1_gates[kc])
                    if kc >= 2:
                        scalar.wait_ge(cV.s, w1v[kc - 2])
                    sop(scalar, "activation", sg[:, :], PA[:, :], AF.Sigmoid,
                        bias=igb_sb[:, kc:kc + 1])
                    p1_sig.append(sop(scalar, "activation", abuf[:, kc, :],
                                      PB[:, :], AF.Sigmoid,
                                      bias=agb_sb[:, kc:kc + 1]))
            if kc >= 1:
                k0 = kc - 1

                @vsec
                def _(vector, k0=k0):
                    sg = [tA, tB][k0 % 2]
                    buf = [tC, bc2][k0 % 2]
                    vector.wait_ge(cS.s, p1_sig[k0])
                    vop(vector, "tensor_tensor", xtb[:, k0, :], buf[:, :],
                        sg[:, :], ALU.mult)
                    vop(vector, "tensor_scalar", abuf[:, k0, :],
                        abuf[:, k0, :], psp[:, k0:k0 + 1], None,
                        ALU.mult, ALU.bypass)
                    vop(vector, "reduce_sum", lsum[:, k0:k0 + 1],
                        abuf[:, k0, :], AX.X)
                w1v.append(cV.v)
            if kc < KC:
                for _t in range(2 if kc % 2 == 0 else 1):
                    emit_tern_piece(len(tn_d))
        v_w1 = cV.v

        @ssec
        def _(scalar):
            scalar.wait_ge(cV.s, v_w1)
            sop(scalar, "activation", acore[:, :], lsum[:, :], AF.Exp)
        s_ac = cS.v

        @ssec
        def _(scalar):
            for kc in range(KC):
                p1_exp.append(sop(scalar, "activation", abuf[:, kc, :],
                                  abuf[:, kc, :], AF.Exp))

        p1_om = []
        p1_sq = []
        p1_xt = []
        for kc in range(KC + 1):
            if kc < KC:
                @vsec
                def _(vector, kc=kc):
                    tmp = [tA, tB][kc % 2]
                    vector.wait_ge(cS.s, p1_exp[kc])
                    vop(vector, "tensor_tensor", tmp[:, :], abuf[:, kc, :],
                        abuf[:, kc, :], ALU.mult)
                    vop(vector, "tensor_scalar", tmp[:, :], tmp[:, :], -1.0,
                        1.0, ALU.mult, ALU.add)
                p1_om.append(cV.v)

                @ssec
                def _(scalar, kc=kc):
                    tmp = [tA, tB][kc % 2]
                    scalar.wait_ge(cV.s, p1_om[kc])
                    p1_sq.append(sop(scalar, "activation", tmp[:, :],
                                     tmp[:, :], AF.Sqrt))
            if kc >= 1:
                k0 = kc - 1

                @vsec
                def _(vector, k0=k0):
                    tmp = [tA, tB][k0 % 2]
                    out = [tC, bc2][k0 % 2]
                    vector.wait_ge(cS.s, p1_sq[k0])
                    vop(vector, "tensor_scalar", tmp[:, 0:1], tmp[:, 0:1],
                        rmv_sb[:, 1:2], rmv_sb[:, 0:1], ALU.mult, ALU.add)
                    vop(vector, "tensor_tensor", xtb[:, k0, :], tmp[:, :],
                        xtb[:, k0, :], ALU.mult)
                    vop(vector, "tensor_tensor_scan", out[:, :],
                        abuf[:, k0, :], xtb[:, k0, :], 0.0, ALU.mult, ALU.add)
                    p1_xt.append(vop(vector, "tensor_copy",
                                     hcore[:, k0:k0 + 1], out[:, T - 1:T]))
        v_p1 = cV.v

        if _PHASES < 2:
            @ysec
            def _(sync):
                sync.wait_ge(cV.s, v_p1)
                sync.wait_ge(dTS2[0].s, dTS2[0].v)
                sync.wait_ge(dTS2[1].s, dTS2[1].v)
                dma(sync, dF, yT[:, :], xT[:, :])
                sync.wait_ge(dF.s, dF.v)
            return nc

        @gsec
        def _(gpsimd):
            gpsimd.wait_ge(cS.s, s_ac)
            gpsimd.wait_ge(cV.s, v_p1)
            dma(gpsimd, dCI, cc1_in[0, 0:2048].rearrange("(p kc) -> p kc",
                                                         kc=KC), acore[:, :])
            dma(gpsimd, dCI, cc1_in[0, 2048:4096].rearrange("(p kc) -> p kc",
                                                            kc=KC), hcore[:, :])
            dma(gpsimd, dCI, cc1_in[0:1, 4096:4160], eye_sb[0:1, 0:64])
            gpsimd.wait_ge(dCI.s, dCI.v)
            gpsimd.collective_compute(
                "AllGather", ALU.bypass, replica_groups=[list(range(NCOR))],
                ins=[cc1_in[:, :]], outs=[cc1_out[:, :]]).then_inc(ccc.s, 1)
            ccc.v += 1
        t_cc1 = ccc.v

        @ysec
        def _(sync):
            sync.wait_ge(ccc.s, t_cc1)
            for k in range(NCOR):
                dma(sync, dU2, exA[:, k, :],
                    cc1_out[k, 0:2048].rearrange("(p kc) -> p kc", kc=KC))
                dma(sync, dU2, exH[:, k, :],
                    cc1_out[k, 2048:4096].rearrange("(p kc) -> p kc", kc=KC))
        t_ccout = dU2.v

        @vsec
        def _(vector):
            vector.wait_ge(dU2.s, t_ccout)
            vop(vector, "tensor_copy", exg[:, :], exH[:, 0, :])
            vop(vector, "tensor_scalar", hin[:, :], exg[:, :], mkv_sb[:, 1:2],
                None, ALU.mult, ALU.bypass)
            for k in range(2, NCOR):
                vop(vector, "tensor_tensor", exg[:, :], exg[:, :],
                    exA[:, k - 1, :], ALU.mult)
                vop(vector, "tensor_tensor", exg[:, :], exg[:, :],
                    exH[:, k - 1, :], ALU.add)
                vop(vector, "scalar_tensor_tensor", hin[:, :], exg[:, :],
                    mkv_sb[:, k:k + 1], hin[:, :], ALU.mult, ALU.add)
        v_hin = cV.v


        if _PHASES < 3:
            @ysec
            def _(sync):
                sync.wait_ge(ccc.s, ccc.v)
                sync.wait_ge(cV.s, v_hin)
                dma(sync, dF, yT[:, :], xT[:, :])
                sync.wait_ge(dF.s, dF.v)
            return nc

        p3_d = []
        p3_y = []
        p3_pair = []
        for kc in range(KC):
            xs = [tC, bc0, bc1, bc2][kc % 4]

            @ysec
            def _(sync, kc=kc, xs=xs):
                if kc < 4:
                    sync.wait_ge(cV.s, v_p1)
                else:
                    sync.wait_ge(cV.s, p3_y[kc - 4])
                dma(sync, dX4[kc % 4], xs[:, :], xT_c[kc])
            p3_d.append(dX4[kc % 4].v)

            @vsec
            def _(vector, kc=kc, xs=xs):
                tmp = [tA, tB][kc % 2]
                vector.wait_ge(dX4[kc % 4].s, p3_d[kc])
                if kc == 0:
                    vector.wait_ge(cV.s, v_hin)
                vop(vector, "tensor_tensor_scan", tmp[:, :], abuf[:, kc, :],
                    xtb[:, kc, :], hin[:, kc:kc + 1], ALU.mult, ALU.add)
                p3_y.append(vop(vector, "tensor_tensor", xmm[:, kc, :],
                                tmp[:, :], xs[:, :], ALU.add))
                vop(vector, "tensor_tensor", tmp[:, :], xmm[:, kc, :],
                    xmm[:, kc, :], ALU.mult)
                if kc == 0:
                    vop(vector, "tensor_copy", PC[:, :], xmm[:, kc, :])
                    vop(vector, "tensor_copy", PD[:, :], tmp[:, :])
                else:
                    vop(vector, "tensor_tensor", PC[:, :], PC[:, :],
                        xmm[:, kc, :], ALU.add)
                    vop(vector, "tensor_tensor", PD[:, :], PD[:, :], tmp[:, :],
                        ALU.add)

            if kc % 2 == 1:
                j = kc // 2

                @gsec
                def _(gpsimd, kc=kc, j=j):
                    gpsimd.wait_ge(cV.s, p3_y[kc])
                    if j >= 2:
                        gpsimd.wait_ge(dOs[j % 2].s, p3_pair[j - 2])
                    dma(gpsimd, dOs[j % 2], yb0p[j],
                        xmm[:, kc - 1:kc + 1, :].rearrange("p a b -> p (a b)"))
                p3_pair.append(dOs[j % 2].v)

        t_p3 = cT.v
        v_p3s = cV.v
        s_p3 = cS.v
        p3_ready = [(dOs[(o // 2) % 2].s, p3_pair[o // 2]) for o in range(OC)]

        for K in range(4):
            @gsec
            def _(gpsimd, K=K):
                gpsimd.wait_ge(dTS2[0].s, agv["wA%d" % K][0])
                gpsimd.wait_ge(dTS2[1].s, agv["wA%d" % K][1])
                gpsimd.collective_compute(
                    "AllGather", ALU.bypass,
                    replica_groups=[list(range(NCOR))],
                    ins=[tsA[K][:, :]], outs=[talA[K][:, :]]).then_inc(
                        ccc.s, 1)
                ccc.v += 1
            agv["a%d" % K] = ccc.v

        @gsec
        def _(gpsimd):
            gpsimd.wait_ge(dTS2[0].s, agv["fin"][0])
            gpsimd.wait_ge(dTS2[1].s, agv["fin"][1])
            gpsimd.collective_compute(
                "AllGather", ALU.bypass, replica_groups=[list(range(NCOR))],
                ins=[tsliceB[:, :]], outs=[tallB[:, :]]).then_inc(ccc.s, 1)
            ccc.v += 1
        agv["b"] = ccc.v

        def emit_ffn(src_sb, srcp, dstp, final, first, glue_waits, src_ready):
            g = emit_ln_glue(waits=glue_waits)

            for kc in range(KC):
                @vsec
                def _(vector, kc=kc):
                    if kc == 0:
                        vector.wait_ge(cV.s, g["v2"])
                    vop(vector, "tensor_tensor", xmm[:, kc, :],
                        src_sb[:, kc, :], bc0[:, :], ALU.subtract)
                    if kc == 0:
                        vop(vector, "tensor_copy", mxa[:, :], xmm[:, 0, :])
                        vop(vector, "tensor_copy", mna[:, :], xmm[:, 0, :])
                    else:
                        vop(vector, "tensor_tensor", mxa[:, :], mxa[:, :],
                            xmm[:, kc, :], ALU.max)
                        vop(vector, "tensor_tensor", mna[:, :], mna[:, :],
                            xmm[:, kc, :], ALU.min)

            @vsec
            def _(vector):
                vop(vector, "tensor_scalar", tA[:, :], mna[:, :], -1.0, None,
                    ALU.mult, ALU.bypass)
                vop(vector, "tensor_tensor", tA[:, :], tA[:, :], mxa[:, :],
                    ALU.max)
            v3 = cV.v

            @tsec
            def _(tensor):
                tensor.wait_ge(cV.s, v3)
                for j in range(8):
                    ttr(tensor, PC[:, j * 128:(j + 1) * 128],
                        tA[:, j * 128:(j + 1) * 128], eye_sb[:, :])
            t1 = cT.v

            @vsec
            def _(vector):
                vector.wait_ge(cT.s, t1)
                for j in range(8):
                    vop(vector, "reduce_max", tB[:, j:j + 1],
                        PC[:, j * 128:(j + 1) * 128], AX.X)
            v4 = cV.v

            @tsec
            def _(tensor):
                tensor.wait_ge(cV.s, v4)
                ttr(tensor, PD[:, 0:128], tB[:, 0:128], eye_sb[:, :])
            t2 = cT.v

            @vsec
            def _(vector):
                vector.wait_ge(cT.s, t2)
                vop(vector, "tensor_copy", tB[0:8, 0:128], PD[0:8, 0:128])
            v5 = cV.v

            @tsec
            def _(tensor):
                tensor.wait_ge(cV.s, v5)
                tmm(tensor, [(PC[:, j * 128:(j + 1) * 128],
                              sel8_sb[:, j * 128:(j + 1) * 128],
                              tB[0:8, 0:128], True, True)
                             for j in range(8)])
            t3 = cT.v

            @vsec
            def _(vector):
                vector.wait_ge(cT.s, t3)
                vop(vector, "tensor_tensor", tA[:, :], bc1[:, :], PC[:, :],
                    ALU.mult)
                vop(vector, "tensor_scalar", tA[:, :], tA[:, :], 1e-5, None,
                    ALU.max, ALU.bypass)
                vop(vector, "reciprocal", tB[:, :], tA[:, :])
                vop(vector, "tensor_tensor", tB[:, :], tB[:, :], bc1[:, :],
                    ALU.mult)
                vop(vector, "tensor_scalar", bc2[:, :], tB[:, :], 127.0, None,
                    ALU.mult, ALU.bypass)
                vop(vector, "tensor_scalar", bc0[:, :], tA[:, :],
                    gam6[:, 3:4], None, ALU.mult, ALU.bypass)
                vop(vector, "tensor_scalar", bc1[:, :], tA[:, :],
                    gam6[:, 4:5], None, ALU.mult, ALU.bypass)
            v6 = cV.v

            @vsec
            def _(vector):
                vector.wait_ge(cV.s, v6)
                for c_ in dWs:
                    vector.wait_ge(c_.s, c_.v)
                for kc in range(KC):
                    vop(vector, "tensor_tensor", tA[:, :], xmm[:, kc, :],
                        bc2[:, :], ALU.mult)
                    vop(vector, "tensor_scalar", actq[:, kc, :], tA[:, :],
                        MAGIC, -MAGIC, ALU.add, ALU.add)
            v_rq = cV.v

            hd_d = []
            hd_t = []
            hd_tg = []
            hd_hm = []
            hd_sl = []
            hd_mn = []
            t_hid_base = cT.v
            m_order = [8 * r + 2 * K + dm
                       for K in range(4) for r in range(MS) for dm in (0, 1)]
            for j, m in enumerate(m_order):
                r, mloc = divmod(m, MS)

                @ysec
                def _(sync, j=j, m=m, r=r, mloc=mloc):
                    if j == 0:
                        if first:
                            sync.wait_ge(dTS2[0].s, agv["fin"][0])
                            sync.wait_ge(dTS2[1].s, agv["fin"][1])
                            sync.wait_ge(cV.s, v_p3s)
                        else:
                            sync.wait_ge(cT.s, t_hid_base)
                    if first and j % 16 == 0:
                        sync.wait_ge(ccc.s, agv["a%d" % (mloc // 2)])
                    caps = []
                    for hf in (0, 1):
                        i = 2 * j + hf
                        if i >= 3:
                            sync.wait_ge(cT.s, hd_t[(i - 3) // 2])
                        base = ((mloc % 2) * 2 + hf) * PIECE
                        dma(sync, dWs[i % 3],
                            wrot[i % 3][:, :, :].rearrange("p a b -> p (a b)"),
                            talA[mloc // 2][r, base:base + PIECE].rearrange(
                                "(p q) -> p q", p=128))
                        caps.append((dWs[i % 3].s, dWs[i % 3].v))
                    hd_d.append(tuple(caps))

                @tsec
                def _(tensor, j=j, m=m):
                    PG = PGV[(2 * j) % 4]
                    PV = PGV[(2 * j + 1) % 4]
                    tensor.wait_ge(*hd_d[j][0])
                    tensor.wait_ge(*hd_d[j][1])
                    if j == 0:
                        tensor.wait_ge(cV.s, v_rq)
                    if j >= 2:
                        tensor.wait_ge(cV.s, hd_hm[j - 2])
                    wg = wrot[(2 * j) % 3]
                    wv = wrot[(2 * j + 1) % 3]
                    g_ = []
                    for kc in range(KC):
                        for hh in range(2):
                            g_.append((PG[:, hh * HL:(hh + 1) * HL],
                                       wg[:, kc, :],
                                       actq[:, kc, hh * HL:(hh + 1) * HL],
                                       kc == 0, kc == KC - 1))
                    for kc in range(KC):
                        for hh in range(2):
                            g_.append((PV[:, hh * HL:(hh + 1) * HL],
                                       wv[:, kc, :],
                                       actq[:, kc, hh * HL:(hh + 1) * HL],
                                       kc == 0, kc == KC - 1))
                    tmm(tensor, g_)
                hd_t.append(cT.v)

                @vsec
                def _(vector, j=j):
                    PG = PGV[(2 * j) % 4]
                    vector.wait_ge(cT.s, hd_t[j])
                    if j >= 1:
                        vector.wait_ge(cS.s, hd_sl[j - 1])
                    hd_tg.append(vop(vector, "tensor_tensor", tA[:, :],
                                     PG[:, :], bc0[:, :], ALU.mult))

                @ssec
                def _(scalar, j=j, m=m):
                    scalar.wait_ge(cV.s, hd_tg[j])
                    sop(scalar, "activation", tC[:, :], tA[:, :], AF.Silu,
                        bias=gb_sb[:, m:m + 1])
                hd_sl.append(cS.v)

                @vsec
                def _(vector, j=j, m=m):
                    PV = PGV[(2 * j + 1) % 4]
                    vector.wait_ge(cS.s, hd_sl[j])
                    vop(vector, "tensor_tensor", tB[:, :], PV[:, :], bc1[:, :],
                        ALU.mult)
                    vop(vector, "scalar_tensor_tensor", hmid[:, m, :], tB[:, :],
                        vb_sb[:, m:m + 1], tC[:, :], ALU.add, ALU.mult)
                    if j == 0:
                        vop(vector, "tensor_copy", mxa[:, :], hmid[:, m, :])
                        vop(vector, "tensor_copy", mna[:, :], hmid[:, m, :])
                    else:
                        vop(vector, "tensor_tensor", mxa[:, :], mxa[:, :],
                            hmid[:, m, :], ALU.max)
                        vop(vector, "tensor_tensor", mna[:, :], mna[:, :],
                            hmid[:, m, :], ALU.min)
                hd_hm.append(cV.v)
            t_hid = cT.v
            v_hid = cV.v

            @vsec
            def _(vector):
                vop(vector, "tensor_scalar", tA[:, :], mna[:, :], -1.0, None,
                    ALU.mult, ALU.bypass)
                vop(vector, "tensor_tensor", tA[:, :], tA[:, :], mxa[:, :],
                    ALU.max)
            v7 = cV.v

            @tsec
            def _(tensor):
                tensor.wait_ge(cV.s, v7)
                for j in range(8):
                    ttr(tensor, PA[:, j * 128:(j + 1) * 128],
                        tA[:, j * 128:(j + 1) * 128], eye_sb[:, :])
            t4 = cT.v

            @vsec
            def _(vector):
                vector.wait_ge(cT.s, t4)
                for j in range(8):
                    vop(vector, "reduce_max", tB[:, j:j + 1],
                        PA[:, j * 128:(j + 1) * 128], AX.X)
            v8 = cV.v

            @tsec
            def _(tensor):
                tensor.wait_ge(cV.s, v8)
                ttr(tensor, PB[:, 0:128], tB[:, 0:128], eye_sb[:, :])
            t5 = cT.v

            @vsec
            def _(vector):
                vector.wait_ge(cT.s, t5)
                vop(vector, "tensor_copy", tB[0:8, 0:128], PB[0:8, 0:128])
            v9 = cV.v

            @tsec
            def _(tensor):
                tensor.wait_ge(cV.s, v9)
                tmm(tensor, [(PA[:, j * 128:(j + 1) * 128],
                              sel8_sb[:, j * 128:(j + 1) * 128],
                              tB[0:8, 0:128], True, True)
                             for j in range(8)])
            t6 = cT.v

            @vsec
            def _(vector):
                vector.wait_ge(cT.s, t6)
                vop(vector, "tensor_scalar", tA[:, :], PA[:, :], 1e-5, None,
                    ALU.max, ALU.bypass)
                vop(vector, "reciprocal", bc2[:, :], tA[:, :])
                vop(vector, "tensor_scalar", bc2[:, :], bc2[:, :], 127.0, None,
                    ALU.mult, ALU.bypass)
                vop(vector, "tensor_scalar", bc0[:, :], tA[:, :],
                    gam6[:, 5:6], None, ALU.mult, ALU.bypass)
            v10 = cV.v

            q3 = []

            @vsec
            def _(vector):
                vector.wait_ge(cV.s, v10)
                for m in range(HC):
                    vop(vector, "tensor_tensor", tA[:, :], hmid[:, m, :],
                        bc2[:, :], ALU.mult)
                    q3.append(vop(vector, "tensor_scalar", hmid[:, m, :],
                                  tA[:, :], MAGIC, -MAGIC, ALU.add, ALU.add))
            v_q3 = cV.v

            dOut = dF if final else dOQ
            dst_ready = []
            op_q = []
            op_d = []
            op_rd = []
            op_t = []
            op_tg = []
            op_y = []
            op_bias = []
            op_sq = []
            op_o = []
            op_ln = []
            for o in range(OC):
                r, oloc = divmod(o, 2)

                @ysec
                def _(sync, o=o, r=r, oloc=oloc):
                    caps = []
                    for qq in range(4):
                        i = 4 * o + qq
                        if i < 3:
                            sync.wait_ge(cT.s, t_hid)
                            if first and i == 0:
                                sync.wait_ge(ccc.s, agv["b"])
                        elif i - 3 >= 4 * o:
                            sync.wait_ge(cT.s, cT.v + (i - 3) - 4 * o + 1)
                        else:
                            sync.wait_ge(cT.s, op_q[i - 3])
                        base = oloc * PIECE * 4 + qq * PIECE
                        dma(sync, dWs[i % 3],
                            wrot[i % 3][:, :, :].rearrange("p a b -> p (a b)"),
                            tallB[r, base:base + PIECE].rearrange(
                                "(p q) -> p q", p=128))
                        caps.append((dWs[i % 3].s, dWs[i % 3].v))
                    op_d.append(caps)
                    if o < 2:
                        sync.wait_ge(cV.s, v_hid)
                    else:
                        sync.wait_ge(cV.s, op_y[o - 2])
                    sync.wait_ge(*src_ready[o])
                    dma(sync, dXs[o % 2], [tC, bc1][o % 2][:, :], half(srcp, o))
                    op_rd.append(dXs[o % 2].v)

                @tsec
                def _(tensor, o=o):
                    PO = [PA, PB][o % 2]
                    if o >= 2:
                        tensor.wait_ge(cV.s, op_tg[o - 2])
                    for qq in range(4):
                        tensor.wait_ge(*op_d[o][qq])
                        if o == 0:
                            tensor.wait_ge(cV.s, q3[16 * (qq + 1) - 1])
                        g_ = []
                        for kk in range(16):
                            kcg = qq * 16 + kk
                            wt = wrot[(4 * o + qq) % 3]
                            for hh in range(2):
                                g_.append((PO[:, hh * HL:(hh + 1) * HL],
                                           wt[:, kk, :],
                                           hmid[:, kcg,
                                                hh * HL:(hh + 1) * HL],
                                           kcg == 0, kcg == HC - 1))
                        tmm(tensor, g_)
                        op_q.append(cT.v)
                op_t.append(cT.v)

                @vsec
                def _(vector, o=o):
                    PO = [PA, PB][o % 2]
                    vector.wait_ge(cT.s, op_t[o])
                    op_tg.append(vop(vector, "tensor_tensor", tA[:, :],
                                     PO[:, :], bc0[:, :], ALU.mult))

                @vsec
                def _(vector, o=o):
                    vector.wait_ge(dXs[o % 2].s, op_rd[o])
                    if o >= 1:
                        vector.wait_ge(dOut.s, op_o[o - 1])
                    op_y.append(vop(vector, "scalar_tensor_tensor", tB[:, :],
                                    tA[:, :], ob_sb[:, o:o + 1],
                                    [tC, bc1][o % 2][:, :], ALU.add, ALU.add))
                    if not final:
                        vop(vector, "tensor_tensor", tA[:, :], tB[:, :],
                            tB[:, :], ALU.mult)
                        if o == 0:
                            vop(vector, "tensor_copy", PC[:, :], tB[:, :])
                            vop(vector, "tensor_copy", PD[:, :], tA[:, :])
                        else:
                            vop(vector, "tensor_tensor", PC[:, :], PC[:, :],
                                tB[:, :], ALU.add)
                            vop(vector, "tensor_tensor", PD[:, :], PD[:, :],
                                tA[:, :], ALU.add)

                @gsec
                def _(gpsimd, o=o):
                    gpsimd.wait_ge(cV.s, op_y[o])
                    dma(gpsimd, dOut,
                        yT_c[o] if final else half(dstp, o), tB[:, :])
                op_o.append(dOut.v)
                dst_ready.append((dOut.s, dOut.v))

            return {"cT": cT.v, "cS": cS.v, "cV": cV.v,
                    "dst_ready": dst_ready}

        def emit_attnb(srcp, dstp, glue_waits, src_ready):
            g = emit_ln_glue(waits=glue_waits)
            t_enter = cT.v
            lands = [tC, bc2, tB]
            lctr = [dXs[0], dXs[1], dX4[2]]

            @gsec
            def _(gpsimd):
                gpsimd.wait_ge(cT.s, t_enter)
                dma(gpsimd, dGu, wbr_b[:, :, :],
                    wbr.rearrange("(kc p) r -> p kc r", p=128))
                dma(gpsimd, dGm, wcm_b[:, :], wcm[:, :])
            t_wb = (dGu.v, dGm.v)

            u_d = []
            u_v = []
            for kc in range(KC):
                @ysec
                def _(sync, kc=kc):
                    if kc < 3:
                        sync.wait_ge(cV.s, g["v2"])
                    else:
                        sync.wait_ge(cV.s, u_v[kc - 3])
                    sync.wait_ge(*src_ready[kc])
                    dma(sync, lctr[kc % 3], lands[kc % 3][:, :],
                        half(srcp, kc))
                u_d.append(lctr[kc % 3].v)

                @vsec
                def _(vector, kc=kc):
                    vector.wait_ge(lctr[kc % 3].s, u_d[kc])
                    if kc == 0:
                        vector.wait_ge(cV.s, g["v2"])
                        for c_ in dWs:
                            vector.wait_ge(c_.s, c_.v)
                    vop(vector, "tensor_tensor", tA[:, :],
                        lands[kc % 3][:, :], bc0[:, :], ALU.subtract)
                    u_v.append(vop(vector, "tensor_tensor", actq[:, kc, :],
                                   tA[:, :], bc1[:, :], ALU.mult))
            v_u = cV.v

            @tsec
            def _(tensor):
                tensor.wait_ge(cV.s, v_u)
                tensor.wait_ge(dGu.s, t_wb[0])
                tensor.wait_ge(dGm.s, t_wb[1])
                g_ = []
                for kc in range(KC):
                    for hh in range(2):
                        g_.append((PA[0:64, hh * HL:(hh + 1) * HL],
                                   wbr_b[:, kc, :],
                                   actq[:, kc, hh * HL:(hh + 1) * HL],
                                   kc == 0, kc == KC - 1))
                tmm(tensor, g_)
            t_t1 = cT.v

            @vsec
            def _(vector):
                vector.wait_ge(cT.s, t_t1)
                vop(vector, "tensor_copy", t1b[:, :], PA[0:64, :])
            v_t1 = cV.v

            @tsec
            def _(tensor):
                tensor.wait_ge(cV.s, v_t1)
                tmm(tensor, halves(PB[0:64], wmid_b[:, :],
                                   lambda hh: t1b[:, hh * HL:(hh + 1) * HL],
                                   True, True))
            t_t2 = cT.v

            @vsec
            def _(vector):
                vector.wait_ge(cT.s, t_t2)
                vop(vector, "tensor_copy", t2b[:, :], PB[0:64, :])
            v_t2 = cV.v

            r_rd = []
            r_t = []
            r_du = []
            r_y = []
            r_pair = []
            for o in range(OC):
                @ysec
                def _(sync, o=o):
                    if o < 3:
                        sync.wait_ge(cV.s, v_u)
                    else:
                        sync.wait_ge(cV.s, r_y[o - 3])
                    sync.wait_ge(*src_ready[o])
                    dma(sync, lctr[o % 3], lands[o % 3][:, :], half(srcp, o))
                r_rd.append(lctr[o % 3].v)

                @tsec
                def _(tensor, o=o):
                    PO = [PA, PB][o % 2]
                    if o == 0:
                        tensor.wait_ge(cV.s, v_t2)
                    if o >= 2:
                        tensor.wait_ge(cV.s, r_du[o - 2])
                    tmm(tensor, halves(PO, wcm_b[:, o * 128:(o + 1) * 128],
                                       lambda hh: t2b[:, hh * HL:(hh + 1) * HL],
                                       True, True))
                r_t.append(cT.v)

                @vsec
                def _(vector, o=o):
                    PO = [PA, PB][o % 2]
                    vector.wait_ge(cT.s, r_t[o])
                    vop(vector, "tensor_scalar", tA[:, :], actq[:, o, :],
                        dv_sb[:, o:o + 1], None, ALU.mult, ALU.bypass)
                    r_du.append(vop(vector, "tensor_tensor", tA[:, :], tA[:, :],
                                    PO[:, :], ALU.add))
                    vector.wait_ge(lctr[o % 3].s, r_rd[o])
                    r_y.append(vop(vector, "tensor_tensor", ymm[:, o, :],
                                   tA[:, :], lands[o % 3][:, :], ALU.add))
                    vop(vector, "tensor_tensor", tA[:, :], ymm[:, o, :],
                        ymm[:, o, :], ALU.mult)
                    if o == 0:
                        vop(vector, "tensor_copy", PC[:, :], ymm[:, o, :])
                        vop(vector, "tensor_copy", PD[:, :], tA[:, :])
                    else:
                        vop(vector, "tensor_tensor", PC[:, :], PC[:, :],
                            ymm[:, o, :], ALU.add)
                        vop(vector, "tensor_tensor", PD[:, :], PD[:, :],
                            tA[:, :], ALU.add)

                if o % 2 == 1:
                    j = o // 2

                    @gsec
                    def _(gpsimd, o=o, j=j):
                        gpsimd.wait_ge(cV.s, r_y[o])
                        if j >= 1:
                            gpsimd.wait_ge(dOQ.s, r_pair[j - 1])
                        dma(gpsimd, dOQ, dstp[j],
                            ymm[:, o - 1:o + 1, :].rearrange(
                                "p a b -> p (a b)"))
                    r_pair.append(dOQ.v)

            return {"cT": cT.v, "cS": cS.v, "cV": cV.v,
                    "dst_ready": [(dOQ.s, r_pair[o // 2]) for o in range(OC)]}

        ffn1 = ab = None
        if _SUBLAYERS >= 2 and _PHASES >= 4:
            ffn1 = emit_ffn(xmm, yb0p, yb1p,
                            final=(_SUBLAYERS == 2), first=True,
                            glue_waits=[(cV.s, v_p3s), (cV.s, cV.v),
                                        (dOs[0].s, dOs[0].v),
                                        (dOs[1].s, dOs[1].v)],
                            src_ready=p3_ready)
        if _SUBLAYERS >= 3 and _PHASES >= 5 and ffn1 is not None:
            ab = emit_attnb(yb1p, yb0p,
                            glue_waits=[(cT.s, ffn1["cT"]), (cS.s, ffn1["cS"]),
                                        (cV.s, cV.v), ffn1["dst_ready"][-1]],
                            src_ready=ffn1["dst_ready"])
        if _SUBLAYERS >= 4 and _PHASES >= 6 and ab is not None:
            emit_ffn(ymm, yb0p, None, final=True, first=False,
                     glue_waits=[(cT.s, ab["cT"]), (cS.s, ab["cS"]),
                                 (cV.s, cV.v), ab["dst_ready"][-1]],
                     src_ready=ab["dst_ready"])

        done_via_ffn = (_SUBLAYERS >= 4 and _PHASES >= 6 and ab is not None) or (
            _SUBLAYERS == 2 and _PHASES >= 4)
        if not done_via_ffn:
            @ysec
            def _(sync):
                sync.wait_ge(dOs[0].s, dOs[0].v)
                sync.wait_ge(dOs[1].s, dOs[1].v)
                sync.wait_ge(dOQ.s, dOQ.v)
                sync.wait_ge(ccc.s, ccc.v)
                use_yb1 = (_SUBLAYERS == 2 and _PHASES >= 4) or (
                    _SUBLAYERS > 2 and _PHASES == 4)
                src = yb1p if use_yb1 else yb0p
                yT_jc = yT.rearrange("(j c p) t -> c j p t", c=2, p=128)
                dma(sync, dF, yT_jc[0], src[:, :, 0:T])
                dma(sync, dF, yT_jc[1], src[:, :, T:2 * T])

        @ysec
        def _(sync):
            sync.wait_ge(dF.s, dF.v)

    return nc


def _host_inputs(inputs):
    x = np.asarray(inputs["L0_attn_a_x"])[0]
    igw = np.ascontiguousarray(np.asarray(inputs["L0_attn_a_ig_w"],
                                          dtype=np.float32))
    agw = np.ascontiguousarray(np.asarray(inputs["L0_attn_a_ag_w"],
                                          dtype=np.float32))
    igb = np.ascontiguousarray(np.asarray(inputs["L0_attn_a_ig_b"],
                                          dtype=np.float32).T)
    agb = np.ascontiguousarray(np.asarray(inputs["L0_attn_a_ag_b"],
                                          dtype=np.float32).T)
    apar = np.ascontiguousarray(
        np.asarray(inputs["L0_attn_a_a_param"], dtype=np.float32
                   ).reshape(KC, 128).T)
    dvec = np.ascontiguousarray(
        np.asarray(inputs["L0_attn_b_D_vec"], dtype=np.float32
                   ).reshape(KC, 128).T)
    gbias = np.ascontiguousarray(
        np.asarray(inputs["L0_ffn_gate_b"], dtype=np.float32).reshape(HC, 128).T)
    vbias = np.ascontiguousarray(
        np.asarray(inputs["L0_ffn_val_b"], dtype=np.float32).reshape(HC, 128).T)
    obias = np.ascontiguousarray(
        np.asarray(inputs["L0_ffn_out_b"], dtype=np.float32).reshape(OC, 128).T)
    gate_wT = np.ascontiguousarray(np.asarray(inputs["L0_ffn_gate_w"],
                                              dtype=np.float32).T)
    val_wT = np.ascontiguousarray(np.asarray(inputs["L0_ffn_val_w"],
                                             dtype=np.float32).T)
    out_wT = np.ascontiguousarray(np.asarray(inputs["L0_ffn_out_w"],
                                             dtype=np.float32).T)
    wbr = np.ascontiguousarray(np.asarray(inputs["L0_attn_b_W_b_rank"],
                                          dtype=np.float32))
    wbsT = np.ascontiguousarray(
        np.asarray(inputs["L0_attn_b_W_b_state"],
                   dtype=np.float32).T.reshape(2, 128, 64))
    wcr = np.ascontiguousarray(
        np.asarray(inputs["L0_attn_b_W_c_rank"],
                   dtype=np.float32).reshape(2, 128, 64))
    wcm = np.ascontiguousarray(np.asarray(inputs["L0_attn_b_W_c_model"],
                                          dtype=np.float32))
    A = np.asarray(inputs["L0_attn_b_A_factors"], dtype=np.float32)

    idx = np.arange(256)
    bit = [((idx >> (7 - l)) & 1) for l in range(8)]
    msk_l = [((idx & ~(1 << (7 - l)))[:, None] ==
              (idx & ~(1 << (7 - l)))[None, :]).astype(np.float32)
             for l in range(8)]
    kgv = np.empty((8, 256, 256), np.float32)
    kmv = np.empty((8, 256, 256), np.float32)
    kgv[0] = A[7][bit[7][:, None], bit[7][None, :]]
    kmv[0] = msk_l[7]
    for i, l in enumerate(range(6, -1, -1)):
        kgv[1 + i] = A[l][bit[l][None, :], bit[l][:, None]]
        kmv[1 + i] = msk_l[l]

    eye = np.eye(128, dtype=np.float32)
    onescol = np.ones((128, 1), np.float32)
    ones_r = np.ones((1, 128), np.float32)
    sel8v = np.ascontiguousarray(np.repeat(np.eye(8, dtype=np.float32), 128,
                                           axis=1))

    def tile_w(wTmat, cols, c):
        sl = wTmat[:, c * cols:(c + 1) * cols]
        nk = wTmat.shape[0] // 128
        t = sl.reshape(nk, 128, cols)
        mloc = cols // 128
        t2 = t.reshape(nk, 128, mloc, 128).transpose(2, 1, 0, 3)
        return np.ascontiguousarray(t2.reshape(mloc, 128, nk * 128))

    maps = []
    for c in range(NCOR):
        xT_ = np.ascontiguousarray(x[c * T:(c + 1) * T, :].T.astype(np.float32))
        rmvv = np.zeros((128, 2), np.float32)
        rmvv[:, 0] = 1.0 if c == 0 else 0.0
        rmvv[:, 1] = 0.0 if c == 0 else 1.0
        mkvv = np.zeros((128, 8), np.float32)
        mkvv[:, c] = 1.0
        def pair_pieces(w8):
            return np.ascontiguousarray(
                w8.reshape(4, 2, 128, 2048).transpose(0, 2, 1, 3).reshape(
                    4, 128, 4096))

        wsg_ = pair_pieces(tile_w(gate_wT, 1024, c))
        wsv_ = pair_pieces(tile_w(val_wT, 1024, c))
        wso4 = tile_w(out_wT, 256, c)
        wso_ = pair_pieces(np.ascontiguousarray(
            wso4.reshape(2, 128, 4, 2048).transpose(0, 2, 1, 3).reshape(
                8, 128, 2048)))
        maps.append({
            "xT": xT_, "igw": igw, "agw": agw, "igb": igb, "agb": agb,
            "apar": apar, "dvec": dvec, "gbias": gbias, "vbias": vbias,
            "obias": obias, "wsg": wsg_, "wsv": wsv_, "wso": wso_,
            "wbr": wbr, "wbsT": wbsT, "wcr": wcr, "wcm": wcm,
            "kg": np.ascontiguousarray(kgv.reshape(8, 2, 128, 256)),
            "km": np.ascontiguousarray(kmv.reshape(8, 2, 128, 256)),
            "eye": eye, "onescol": onescol, "ones_r": ones_r,
            "rmv": rmvv, "mkv": mkvv, "sel8": sel8v,
        })
    return maps


def kernel(**inputs):
    maps = _host_inputs(inputs)
    nc = build_nc()
    res = run_bass_kernel_spmd(nc, maps, core_ids=list(range(NCOR)))
    out = np.empty((1, NCOR * T, D), np.float32)
    for c in range(NCOR):
        out[0, c * T:(c + 1) * T, :] = res.results[c]["yT"].T
    return out


# version 08
# speedup vs baseline: 1.0623x; 1.0623x over previous
import os as _os
import numpy as np
from contextlib import ExitStack

import concourse.bass as bass
import concourse.mybir as mybir
from concourse.bass_utils import run_bass_kernel_spmd

F32 = mybir.dt.float32
BF16 = mybir.dt.bfloat16
AF = mybir.ActivationFunctionType
ALU = mybir.AluOpType
AX = mybir.AxisListType

MAGIC = 1.5 * 2.0**23
NCOR = 8
T = 1024
D = 2048
KC = 16
H = 8192
HC = 64
OC = 16
MS = 8
NW = float(H * D)
PIECE = 128 * 2048
SLICE_ELEMS = 24 * PIECE
GATE_OFF = 0
VAL_OFF = 8 * PIECE
OUT_OFF = 16 * PIECE
CC1_N = 4160
HL = 512

_SUBLAYERS = 4
_PHASES = int(_os.environ.get("KPHASES", "99"))


class Ctr:
    __slots__ = ("s", "v")

    def __init__(self, s):
        self.s = s
        self.v = 0


def build_nc():
    nc = bass.Bass("TRN2", target_bir_lowering=False, debug=False)

    def inp(name, shape, dt=F32):
        return nc.dram_tensor(name, list(shape), dt, kind="ExternalInput").ap()

    xT = inp("xT", [D, T])
    igw = inp("igw", [KC, 128, 128])
    agw = inp("agw", [KC, 128, 128])
    igb = inp("igb", [128, KC])
    agb = inp("agb", [128, KC])
    apar = inp("apar", [128, KC])
    dvec = inp("dvec", [128, KC])
    gbias = inp("gbias", [128, HC])
    vbias = inp("vbias", [128, HC])
    obias = inp("obias", [128, OC])
    wsg = inp("wsg", [MS // 2, 128, 4096])
    wsv = inp("wsv", [MS // 2, 128, 4096])
    wso = inp("wso", [MS // 2, 128, 4096])
    wbr = inp("wbr", [D, 64])
    wbsT = inp("wbsT", [2, 128, 64])
    wcr = inp("wcr", [2, 128, 64])
    wcm = inp("wcm", [64, D])
    kg = inp("kg", [8, 2, 128, 256])
    km = inp("km", [8, 2, 128, 256])
    eye = inp("eye", [128, 128])
    onescol = inp("onescol", [128, 1])
    ones_r = inp("ones_r", [1, 128])
    rmv = inp("rmv", [128, 2])
    mkv = inp("mkv", [128, 8])
    sel8 = inp("sel8", [8, 1024])

    yT = nc.dram_tensor("yT", [D, T], F32, kind="ExternalOutput").ap()

    cc1_in = nc.dram_tensor("cc1_in", [1, CC1_N], F32).ap()
    cc1_out = nc.dram_tensor("cc1_out", [NCOR, CC1_N], F32,
                             addr_space="Shared").ap()
    tsA = [nc.dram_tensor(f"tsA{K}", [1, 4 * PIECE], BF16).ap()
           for K in range(4)]
    talA = [nc.dram_tensor(f"talA{K}", [NCOR, 4 * PIECE], BF16,
                           addr_space="Shared").ap() for K in range(4)]
    tsliceB = nc.dram_tensor("tsliceB", [1, 8 * PIECE], BF16).ap()
    cc0_in = nc.dram_tensor("cc0_in", [1, 64], F32).ap()
    cc0_out = nc.dram_tensor("cc0_out", [NCOR, 64], F32,
                             addr_space="Shared").ap()
    tallB = nc.dram_tensor("tallB", [NCOR, 8 * PIECE], BF16,
                           addr_space="Shared").ap()
    yb0p = nc.dram_tensor("yb0p", [KC // 2, 128, 2 * T], F32).ap()
    yb1p = nc.dram_tensor("yb1p", [KC // 2, 128, 2 * T], F32).ap()

    xT_c = xT.rearrange("(kc p) t -> kc p t", p=128)
    yT_c = yT.rearrange("(kc p) t -> kc p t", p=128)

    def half(buf, o):
        return buf[o // 2, :, (o % 2) * T:(o % 2) * T + T]

    with ExitStack() as ctx:
        AT = lambda n, s, d, off: nc.alloc_sbuf_tensor_at(n, s, d, offset=off)

        wrot = [AT("wrot0", [128, KC, 128], BF16, 52224),
                AT("wrot1", [128, KC, 128], BF16, 56320),
                AT("wrot2", [128, KC, 128], BF16, 60416)]
        wbr_b = AT("wbr_b", [128, KC, 64], BF16, 52224)
        wcm_b = AT("wcm_b", [64, D], BF16, 56320)
        tstg = [AT("tstg0", [128, 2048], BF16, 52224),
                AT("tstg1", [128, 2048], BF16, 56320)]

        actq = AT("actq", [128, KC, T], BF16, 64512)
        wstage = [AT("wstage0", [128, 2048], F32, 64512),
                  AT("wstage1", [128, 2048], F32, 72704),
                  AT("wstage2", [128, 2048], F32, 80896)]
        igw_sb = AT("igw_sb", [128, KC, 128], F32, 80896)
        agw_sb = AT("agw_sb", [128, KC, 128], F32, 89088)
        wrot6 = wrot + [AT("wrot3", [128, KC, 128], BF16, 80896),
                        AT("wrot4", [128, KC, 128], BF16, 84992),
                        AT("wrot5", [128, KC, 128], BF16, 89088)]
        exA = AT("exA", [128, NCOR, KC], F32, 60416)
        exH = AT("exH", [128, NCOR, KC], F32, 60928)
        hin = AT("hin", [128, KC], F32, 61440)
        hcore = AT("hcore", [128, KC], F32, 61504)
        lsum = AT("lsum", [128, KC], F32, 61568)
        acore = AT("acore", [128, KC], F32, 61632)
        exg = AT("exg", [128, KC], F32, 61696)
        st8 = AT("st8", [8, 16], F32, 61760)
        wst3 = AT("wst3", [128, 4], F32, 61824)

        abuf = AT("abuf", [128, KC, T], F32, 97280)
        wpair = [AT("wpair0", [128, 4096], F32, 113664),
                 AT("wpair1", [128, 4096], F32, 130048)]
        xtb = AT("xtb", [128, KC, T], F32, 162816)
        hmid = AT("hmid", [128, HC, T], BF16, 97280)
        xmm = AT("xmm", [128, KC, T], F32, 97280)
        ymm = AT("ymm", [128, KC, T], F32, 101376)
        k_u = AT("k_u", [128, 2, 256], F32, 97280)
        k_m = AT("k_m", [128, 2, 256], F32, 99328)
        k_l = AT("k_l", [128, 2, 256], F32, 101376)
        k_r = [AT("k_r0", [128, 2, 256], F32, 103424),
               AT("k_r1", [128, 2, 256], F32, 105472)]
        k_s1 = AT("k_s1", [64, 256], F32, 107520)
        k_s1T = AT("k_s1T", [128, 2, 64], F32, 108544)
        t1b = AT("t1b", [64, T], BF16, 97280)
        t2b = AT("t2b", [64, T], BF16, 99328)
        wbsT_sb = AT("wbsT_sb", [128, 2, 64], F32, 109568)
        wcr_sb = AT("wcr_sb", [128, 2, 64], F32, 110080)
        wmid_f = AT("wmid_f", [64, 64], F32, 110592)

        sb = lambda n, s, d: ctx.enter_context(nc.sbuf_tensor(n, s, d))
        tA = sb("tA", [128, T], F32)
        tB = sb("tB", [128, T], F32)
        tC = sb("tC", [128, T], F32)
        bc0 = sb("bc0", [128, T], F32)
        bc1 = sb("bc1", [128, T], F32)
        bc2 = sb("bc2", [128, T], F32)
        mxa = sb("mxa", [128, T], BF16)
        mna = sb("mna", [128, T], BF16)
        gb_sb = sb("gb_sb", [128, HC], F32)
        vb_sb = sb("vb_sb", [128, HC], F32)
        ob_sb = sb("ob_sb", [128, OC], F32)
        igb_sb = sb("igb_sb", [128, KC], F32)
        agb_sb = sb("agb_sb", [128, KC], F32)
        psp = sb("psp", [128, KC], F32)
        dv_sb = sb("dv_sb", [128, KC], F32)
        eye_sb = sb("eye_sb", [128, 128], F32)
        onesc_sb = sb("onesc_sb", [128, 1], F32)
        onesr_sb = sb("onesr_sb", [1, 128], F32)
        rmv_sb = sb("rmv_sb", [128, 2], F32)
        mkv_sb = sb("mkv_sb", [128, 8], F32)
        gam6 = sb("gam6", [128, 8], F32)
        wmid_b = sb("wmid_b", [64, 64], BF16)
        sel8_sb = sb("sel8_sb", [8, 1024], F32)

        assert nc.sbuf_base <= 52224, f"auto SBUF overflow: base={nc.sbuf_base}"

        PA = ctx.enter_context(nc.psum_tensor("PA", [128, T], F32))
        PB = ctx.enter_context(nc.psum_tensor("PB", [128, T], F32))
        PC = ctx.enter_context(nc.psum_tensor("PC", [128, T], F32))
        PD = ctx.enter_context(nc.psum_tensor("PD", [128, T], F32))
        PGV = [PA, PB, PC, PD]

        sem = lambda n: ctx.enter_context(nc.semaphore(n))
        dC = Ctr(sem("dC"))
        dXs = [Ctr(sem("dX0")), Ctr(sem("dX1"))]
        dX4 = dXs + [Ctr(sem("dX2")), Ctr(sem("dX3"))]
        dWs = [Ctr(sem("dW0")), Ctr(sem("dW1")), Ctr(sem("dW2"))]
        dWp = [Ctr(sem("dWp0")), Ctr(sem("dWp1"))]
        dQx = [Ctr(sem("dQx0")), Ctr(sem("dQx1"))]
        dGu = Ctr(sem("dGu"))
        dGm = Ctr(sem("dGm"))
        dOs = [Ctr(sem("dO0")), Ctr(sem("dO1"))]
        dOQ = Ctr(sem("dOQ"))
        dTS2 = [Ctr(sem("dTSa")), Ctr(sem("dTSb"))]
        dCI = Ctr(sem("dCI"))
        dU1 = Ctr(sem("dU1"))
        dU2 = Ctr(sem("dU2"))
        dF = Ctr(sem("dF"))
        cT = Ctr(sem("cT"))
        cV = Ctr(sem("cV"))
        cS = Ctr(sem("cS"))
        cG = Ctr(sem("cG"))
        ccc = Ctr(sem("ccc"))

        block = ctx.enter_context(nc.Block())

        def vop(h, name, *a, **kw):
            h.wait_ge(cV.s, cV.v)
            getattr(h, name)(*a, **kw).then_inc(cV.s, 1)
            cV.v += 1
            return cV.v

        def sop(h, name, *a, **kw):
            h.wait_ge(cS.s, cS.v)
            getattr(h, name)(*a, **kw).then_inc(cS.s, 1)
            cS.v += 1
            return cS.v

        def gop(h, name, *a, **kw):
            h.wait_ge(cG.s, cG.v)
            getattr(h, name)(*a, **kw).then_inc(cG.s, 1)
            cG.v += 1
            return cG.v

        def dma(h, ctr, out, in_, **kw):
            h.dma_start(out=out, in_=in_, **kw).then_inc(ctr.s, 16)
            ctr.v += 16
            return ctr.v

        def tmm(h, mms):
            for i, (o, l, r, st, sp) in enumerate(mms):
                ins = h.matmul(o, l, r, start=st, stop=sp)
                if i == len(mms) - 1:
                    ins.then_inc(cT.s, 1)
                    cT.v += 1
            return cT.v

        def ttr(h, out, in_, ident):
            h.transpose(out, in_, ident).then_inc(cT.s, 1)
            cT.v += 1
            return cT.v

        vsec = block.vector
        ssec = block.scalar
        tsec = block.tensor
        gsec = block.gpsimd
        ysec = block.sync

        def halves(P_, lhsT, rhs_fn, st, sp):
            return [(P_[:, hh * HL:(hh + 1) * HL], lhsT, rhs_fn(hh), st, sp)
                    for hh in range(2)]

        consts = [
            (igb_sb[:, :], igb), (agb_sb[:, :], agb), (psp[:, :], apar),
            (dv_sb[:, :], dvec), (gb_sb[:, :], gbias), (vb_sb[:, :], vbias),
            (ob_sb[:, :], obias), (eye_sb[:, :], eye), (onesc_sb[:, :], onescol),
            (onesr_sb[:, :], ones_r), (rmv_sb[:, :], rmv), (mkv_sb[:, :], mkv),
            (sel8_sb[:, :], sel8),
            (wbsT_sb[:, :, :], wbsT.rearrange("b p r -> p b r")),
            (wcr_sb[:, :, :], wcr.rearrange("b p r -> p b r")),
        ]

        @ysec
        def _(sync):
            for dst, src in consts:
                dma(sync, dC, dst, src)
        t_cst = dC.v

        @ssec
        def _(scalar):
            scalar.wait_ge(dC.s, t_cst)
            sop(scalar, "activation", psp[:, :], psp[:, :], AF.Exp)
            sop(scalar, "activation", psp[:, :], psp[:, :], AF.Ln, bias=1.0)
        s_psp0 = cS.v

        @vsec
        def _(vector):
            vector.wait_ge(cS.s, s_psp0)
            vop(vector, "tensor_scalar", psp[:, :], psp[:, :], -8.0, None,
                ALU.mult, ALU.bypass)
        v_psp = cV.v

        kg_r = kg.rearrange("l b p c -> l p b c")
        km_r = km.rearrange("l b p c -> l p b c")
        kdma_t = []
        kv_t = []
        kt_t = []
        for i in range(8):
            @gsec
            def _(gpsimd, i=i):
                if i >= 1:
                    gpsimd.wait_ge(cV.s, kv_t[i - 1][0])
                dma(gpsimd, dGu, k_u[:, :, :], kg_r[i])
                dma(gpsimd, dGm, k_m[:, :, :], km_r[i])
            kdma_t.append((dGu.v, dGm.v))

            @vsec
            def _(vector, i=i):
                vector.wait_ge(dGu.s, kdma_t[i][0])
                vector.wait_ge(dGm.s, kdma_t[i][1])
                if i >= 2:
                    vector.wait_ge(cT.s, kt_t[i - 1])
                dst = k_r[0] if i == 0 else k_l
                lb = vop(vector, "tensor_tensor", dst[:, :, :], k_u[:, :, :],
                         k_m[:, :, :], ALU.mult)
                kv_t.append([lb])

            if i >= 1:
                @tsec
                def _(tensor, i=i):
                    tensor.wait_ge(cV.s, kv_t[i][0])
                    src = k_r[(i - 1) % 2]
                    tmm(tensor, [
                        ([PA, PB][mblk][:, 0:256],
                         k_l[:, kblk, mblk * 128:(mblk + 1) * 128],
                         src[:, kblk, :], kblk == 0, kblk == 1)
                        for mblk in range(2) for kblk in range(2)])
                kt_t.append(cT.v)

                @vsec
                def _(vector, i=i):
                    vector.wait_ge(cT.s, kt_t[i])
                    out = k_r[i % 2]
                    vop(vector, "tensor_copy", out[:, 0, :], PA[:, 0:256])
                    ev = vop(vector, "tensor_copy", out[:, 1, :], PB[:, 0:256])
                    kv_t[i].append(ev)
            else:
                kt_t.append(None)

        @tsec
        def _(tensor):
            tensor.wait_ge(cV.s, kv_t[7][1])
            tensor.wait_ge(dC.s, t_cst)
            tmm(tensor, [(PA[0:64, 0:256], wbsT_sb[:, kb, :], k_r[1][:, kb, :],
                          kb == 0, kb == 1) for kb in range(2)])
        t_s1 = cT.v

        @vsec
        def _(vector):
            vector.wait_ge(cT.s, t_s1)
            vop(vector, "tensor_copy", k_s1[:, :], PA[0:64, 0:256])
        v_s1 = cV.v

        @tsec
        def _(tensor):
            tensor.wait_ge(cV.s, v_s1)
            ttr(tensor, PB[:, 0:64], k_s1[:, 0:128], eye_sb[0:64, 0:64])
        t_tr1 = cT.v

        @vsec
        def _(vector):
            vector.wait_ge(cT.s, t_tr1)
            vop(vector, "tensor_copy", k_s1T[:, 0, :], PB[:, 0:64])
        v_tr1 = cV.v

        @tsec
        def _(tensor):
            tensor.wait_ge(cV.s, v_tr1)
            ttr(tensor, PB[:, 64:128], k_s1[:, 128:256], eye_sb[0:64, 0:64])
        t_tr2 = cT.v

        @vsec
        def _(vector):
            vector.wait_ge(cT.s, t_tr2)
            vop(vector, "tensor_copy", k_s1T[:, 1, :], PB[:, 64:128])
        v_tr2 = cV.v

        @tsec
        def _(tensor):
            tensor.wait_ge(cV.s, v_tr2)
            tmm(tensor, [(PA[0:64, 0:64], k_s1T[:, jb, :], wcr_sb[:, jb, :],
                          jb == 0, jb == 1) for jb in range(2)])
        t_wm = cT.v

        @vsec
        def _(vector):
            vector.wait_ge(cT.s, t_wm)
            vop(vector, "tensor_copy", wmid_f[:, :], PA[0:64, 0:64])
            vop(vector, "tensor_copy", wmid_b[:, :], wmid_f[:, :])
        v_wm = cV.v

        x0_d = []
        x0_v = []
        ws_d = []
        ws_v = []

        def emit_ws_piece(i):
            w, piece = divmod(i, 4)

            @gsec
            def _(gpsimd, i=i, w=w, piece=piece):
                if i >= 2:
                    gpsimd.wait_ge(cV.s, ws_v[i - 2])
                dma(gpsimd, dWp[i % 2], wpair[i % 2][:, :],
                    [wsg, wsv, wso][w][piece])
            ws_d.append(dWp[i % 2].v)

            @vsec
            def _(vector, i=i, w=w):
                vector.wait_ge(dWp[i % 2].s, ws_d[i])
                if i % 4 == 0:
                    vop(vector, "reduce_sum", wst3[:, w:w + 1],
                        wpair[i % 2][:, :], AX.X, apply_absolute_value=True)
                else:
                    vop(vector, "reduce_sum", wst3[:, 3:4], wpair[i % 2][:, :],
                        AX.X, apply_absolute_value=True)
                    vop(vector, "tensor_tensor", wst3[:, w:w + 1],
                        wst3[:, w:w + 1], wst3[:, 3:4], ALU.add)
            ws_v.append(cV.v)

        for kc in range(KC):
            @ysec
            def _(sync, kc=kc):
                if kc >= 4:
                    sync.wait_ge(cV.s, x0_v[kc - 4])
                dma(sync, dX4[kc % 4], xtb[:, kc, :], xT_c[kc])
            x0_d.append(dX4[kc % 4].v)

            @vsec
            def _(vector, kc=kc):
                sq = [tA, tB][kc % 2]
                vector.wait_ge(dX4[kc % 4].s, x0_d[kc])
                vop(vector, "tensor_tensor", sq[:, :], xtb[:, kc, :],
                    xtb[:, kc, :], ALU.mult)
                if kc == 0:
                    vop(vector, "tensor_copy", PC[:, :], xtb[:, kc, :])
                    vop(vector, "tensor_copy", PD[:, :], sq[:, :])
                else:
                    vop(vector, "tensor_tensor", PC[:, :], PC[:, :],
                        xtb[:, kc, :], ALU.add)
                    vop(vector, "tensor_tensor", PD[:, :], PD[:, :], sq[:, :],
                        ALU.add)
            x0_v.append(cV.v)
            if kc % 4 != 3:
                emit_ws_piece(len(ws_d))
        v_ln1 = max(x0_v)

        if _PHASES < 1:
            @ysec
            def _(sync):
                sync.wait_ge(cV.s, max(v_wm, ws_v[-1], v_ln1))
                dma(sync, dF, yT[:, :], xT[:, :])
                sync.wait_ge(dF.s, dF.v)
            return nc

        def emit_ln_glue(waits):
            b = {}

            @vsec
            def _(vector):
                for w_ in waits:
                    vector.wait_ge(*w_)
                vop(vector, "tensor_copy", tA[:, :], PC[:, :])
                vop(vector, "tensor_copy", tB[:, :], PD[:, :])
            b["vp"] = cV.v

            @tsec
            def _(tensor):
                tensor.wait_ge(cV.s, b["vp"])
                g = []
                for hh in range(2):
                    g.append((PC[0:1, hh * HL:(hh + 1) * HL], onesc_sb[:, 0:1],
                              tA[:, hh * HL:(hh + 1) * HL], True, True))
                    g.append((PD[0:1, hh * HL:(hh + 1) * HL], onesc_sb[:, 0:1],
                              tB[:, hh * HL:(hh + 1) * HL], True, True))
                tmm(tensor, g)
            b["tp"] = cT.v

            @vsec
            def _(vector):
                vector.wait_ge(cT.s, b["tp"])
                vop(vector, "tensor_copy", tA[0:1, :], PC[0:1, :])
                vop(vector, "tensor_scalar", tA[0:1, :], tA[0:1, :], 1.0 / D,
                    None, ALU.mult, ALU.bypass)
                vop(vector, "tensor_copy", tB[0:1, :], PD[0:1, :])
                vop(vector, "tensor_scalar", tB[0:1, :], tB[0:1, :], 1.0 / D,
                    None, ALU.mult, ALU.bypass)
                vop(vector, "tensor_tensor", tC[0:1, :], tA[0:1, :], tA[0:1, :],
                    ALU.mult)
                vop(vector, "tensor_tensor", tB[0:1, :], tB[0:1, :], tC[0:1, :],
                    ALU.subtract)
                vop(vector, "tensor_scalar", tB[0:1, :], tB[0:1, :], 1e-5, None,
                    ALU.add, ALU.bypass)
            b["v0"] = cV.v

            @ssec
            def _(scalar):
                scalar.wait_ge(cV.s, b["v0"])
                sop(scalar, "activation", tB[0:1, :], tB[0:1, :], AF.Sqrt)
            b["s0"] = cS.v

            @vsec
            def _(vector):
                vector.wait_ge(cS.s, b["s0"])
                vop(vector, "reciprocal", tB[0:1, :], tB[0:1, :])
            b["v1"] = cV.v

            @tsec
            def _(tensor):
                tensor.wait_ge(cV.s, b["v1"])
                g = []
                for hh in range(2):
                    g.append((PC[:, hh * HL:(hh + 1) * HL], onesr_sb[0:1, :],
                              tA[0:1, hh * HL:(hh + 1) * HL], True, True))
                    g.append((PD[:, hh * HL:(hh + 1) * HL], onesr_sb[0:1, :],
                              tB[0:1, hh * HL:(hh + 1) * HL], True, True))
                tmm(tensor, g)
            b["t0"] = cT.v

            @vsec
            def _(vector):
                vector.wait_ge(cT.s, b["t0"])
                vop(vector, "tensor_copy", bc0[:, :], PC[:, :])
                vop(vector, "tensor_copy", bc1[:, :], PD[:, :])
            b["v2"] = cV.v
            return b

        @tsec
        def _(tensor):
            tensor.wait_ge(cV.s, ws_v[-1])
            tensor.wait_ge(cV.s, v_wm)
            tmm(tensor, [(PB[0:1, 0:4], onesc_sb[:, 0:1], wst3[:, 0:4],
                          True, True)])
        t_st = cT.v

        @vsec
        def _(vector):
            vector.wait_ge(cT.s, t_st)
            vop(vector, "tensor_copy", st8[0:1, 0:4], PB[0:1, 0:4])
        v_st = cV.v

        @gsec
        def _(gpsimd):
            gpsimd.wait_ge(cV.s, v_st)
            dma(gpsimd, dCI, cc0_in[0:1, 0:4], st8[0:1, 0:4])
            dma(gpsimd, dCI, cc0_in[0:1, 4:64], eye_sb[0:1, 0:60])
            gpsimd.wait_ge(dCI.s, dCI.v)
            gpsimd.collective_compute(
                "AllGather", ALU.bypass, replica_groups=[list(range(NCOR))],
                ins=[cc0_in[:, :]], outs=[cc0_out[:, :]]).then_inc(ccc.s, 1)
            ccc.v += 1
        t_cc0 = ccc.v

        @ysec
        def _(sync):
            sync.wait_ge(ccc.s, t_cc0)
            dma(sync, dU1, st8[:, 0:4], cc0_out[:, 0:4],
                allow_slow_non_contiguous=True)
        t_st8 = dU1.v

        @tsec
        def _(tensor):
            tensor.wait_ge(dU1.s, t_st8)
            tmm(tensor, [(PB[0:1, 4:8], onesc_sb[0:8, 0:1], st8[:, 0:4],
                          True, True)])
        t_tot = cT.v

        @vsec
        def _(vector):
            vector.wait_ge(cT.s, t_tot)
            vop(vector, "tensor_scalar", st8[0:1, 4:8], PB[0:1, 4:8], 1.0 / NW,
                1e-5, ALU.mult, ALU.max)
            vop(vector, "reciprocal", st8[0:1, 8:12], st8[0:1, 4:8])
            vop(vector, "tensor_scalar", st8[0:1, 12:16], st8[0:1, 4:8],
                1.0 / 127.0, None, ALU.mult, ALU.bypass)
        v_gam = cV.v

        @tsec
        def _(tensor):
            tensor.wait_ge(cV.s, v_gam)
            tmm(tensor, [(PB[:, 8 + j:9 + j], onesr_sb[0:1, :],
                          st8[0:1, c:c + 1], True, True)
                         for j, c in enumerate([8, 9, 10, 12, 13, 14])])
        t_rep = cT.v

        @vsec
        def _(vector):
            vector.wait_ge(cT.s, t_rep)
            vop(vector, "tensor_copy", gam6[:, 0:6], PB[:, 8:14])
        v_rep = cV.v

        g1 = emit_ln_glue(waits=[(cV.s, v_ln1), (cS.s, cS.v), (cV.s, cV.v)])

        @ysec
        def _(sync):
            sync.wait_ge(cV.s, ws_v[-1])
            dma(sync, dC, igw_sb[:, :, :], igw.rearrange("k a b -> a k b"))
            dma(sync, dC, agw_sb[:, :, :], agw.rearrange("k a b -> a k b"))
        t_gw = dC.v

        tn_d = []
        tn_v = []
        tn_o = []
        agv = {}

        def emit_tern_piece(i):
            w, piece = divmod(i, MS)

            @ysec
            def _(sync, i=i, w=w, piece=piece):
                if i < 2:
                    sync.wait_ge(cV.s, ws_v[-1])
                else:
                    sync.wait_ge(cV.s, tn_v[i - 2])
                src = [wsg, wsv, wso][w][piece // 2][
                    :, (piece % 2) * 2048:(piece % 2) * 2048 + 2048]
                dma(sync, dWs[i % 2], wstage[i % 2][:, :], src)
            tn_d.append(dWs[i % 2].v)

            @vsec
            def _(vector, i=i, w=w):
                vector.wait_ge(dWs[i % 2].s, tn_d[i])
                if i == 0:
                    vector.wait_ge(cV.s, v_rep)
                if i >= 2:
                    vector.wait_ge(dTS2[i % 2].s, tn_o[i - 2])
                vop(vector, "tensor_scalar", wstage[i % 2][:, :],
                    wstage[i % 2][:, :], gam6[:, w:w + 1], 1.49,
                    ALU.mult, ALU.min)
                vop(vector, "tensor_scalar", wstage[i % 2][:, :],
                    wstage[i % 2][:, :], -1.49, MAGIC, ALU.max, ALU.add)
                vop(vector, "tensor_scalar", tstg[i % 2][:, :],
                    wstage[i % 2][:, :], -MAGIC, None, ALU.add, ALU.bypass)
            tn_v.append(cV.v)

            @gsec
            def _(gpsimd, i=i, w=w, piece=piece):
                gpsimd.wait_ge(cV.s, tn_v[i])
                if i < 16:
                    idx = (piece % 2) * 2 + w
                    dst = tsA[piece // 2][0, idx * PIECE:(idx + 1) * PIECE]
                else:
                    dst = tsliceB[0, (i - 16) * PIECE:(i - 15) * PIECE]
                dma(gpsimd, dTS2[i % 2], dst.rearrange("(p q) -> p q", p=128),
                    tstg[i % 2][:, :])
            tn_o.append(dTS2[i % 2].v)

            if i >= 9 and i % 2 == 1 and i <= 15:
                agv["wA%d" % ((i - 9) // 2)] = (dTS2[0].v, dTS2[1].v)
            if i == 23:
                agv["fin"] = (dTS2[0].v, dTS2[1].v)

        p1_xln = []
        p1_gates = []
        p1_sig = []
        p1_exp = []
        w1v = []
        for kc in range(KC + 1):
            if kc < KC:
                @vsec
                def _(vector, kc=kc):
                    buf = [tC, bc2][kc % 2]
                    if kc == 0:
                        vector.wait_ge(cV.s, g1["v2"])
                    vop(vector, "tensor_tensor", buf[:, :], xtb[:, kc, :],
                        bc0[:, :], ALU.subtract)
                    vop(vector, "tensor_tensor", buf[:, :], buf[:, :],
                        bc1[:, :], ALU.mult)
                p1_xln.append(cV.v)

                @tsec
                def _(tensor, kc=kc):
                    buf = [tC, bc2][kc % 2]
                    tensor.wait_ge(cV.s, p1_xln[kc])
                    if kc == 0:
                        tensor.wait_ge(dC.s, t_gw)
                        tensor.wait_ge(cV.s, v_wm)
                    if kc >= 1:
                        tensor.wait_ge(cS.s, p1_sig[kc - 1])
                    tmm(tensor, halves(PA, igw_sb[:, kc, :],
                                       lambda hh: buf[:, hh * HL:(hh + 1) * HL],
                                       True, True)
                        + halves(PB, agw_sb[:, kc, :],
                                 lambda hh: buf[:, hh * HL:(hh + 1) * HL],
                                 True, True))
                p1_gates.append(cT.v)

                @ssec
                def _(scalar, kc=kc):
                    sg = [tA, tB][kc % 2]
                    scalar.wait_ge(cT.s, p1_gates[kc])
                    if kc >= 2:
                        scalar.wait_ge(cV.s, w1v[kc - 2])
                    sop(scalar, "activation", sg[:, :], PA[:, :], AF.Sigmoid,
                        bias=igb_sb[:, kc:kc + 1])
                    p1_sig.append(sop(scalar, "activation", abuf[:, kc, :],
                                      PB[:, :], AF.Sigmoid,
                                      bias=agb_sb[:, kc:kc + 1]))
            if kc >= 1:
                k0 = kc - 1

                @vsec
                def _(vector, k0=k0):
                    sg = [tA, tB][k0 % 2]
                    buf = [tC, bc2][k0 % 2]
                    vector.wait_ge(cS.s, p1_sig[k0])
                    vop(vector, "tensor_tensor", xtb[:, k0, :], buf[:, :],
                        sg[:, :], ALU.mult)
                    vop(vector, "tensor_scalar", abuf[:, k0, :],
                        abuf[:, k0, :], psp[:, k0:k0 + 1], None,
                        ALU.mult, ALU.bypass)
                    vop(vector, "reduce_sum", lsum[:, k0:k0 + 1],
                        abuf[:, k0, :], AX.X)
                w1v.append(cV.v)
            if kc < KC:
                for _t in range(2 if kc % 2 == 0 else 1):
                    emit_tern_piece(len(tn_d))
        v_w1 = cV.v

        @ssec
        def _(scalar):
            scalar.wait_ge(cV.s, v_w1)
            sop(scalar, "activation", acore[:, :], lsum[:, :], AF.Exp)
        s_ac = cS.v

        @ssec
        def _(scalar):
            for kc in range(KC):
                p1_exp.append(sop(scalar, "activation", abuf[:, kc, :],
                                  abuf[:, kc, :], AF.Exp))

        p1_om = []
        p1_sq = []
        p1_xt = []
        for kc in range(KC + 1):
            if kc < KC:
                @vsec
                def _(vector, kc=kc):
                    tmp = [tA, tB][kc % 2]
                    vector.wait_ge(cS.s, p1_exp[kc])
                    vop(vector, "tensor_tensor", tmp[:, :], abuf[:, kc, :],
                        abuf[:, kc, :], ALU.mult)
                    vop(vector, "tensor_scalar", tmp[:, :], tmp[:, :], -1.0,
                        1.0, ALU.mult, ALU.add)
                p1_om.append(cV.v)

                @ssec
                def _(scalar, kc=kc):
                    tmp = [tA, tB][kc % 2]
                    scalar.wait_ge(cV.s, p1_om[kc])
                    p1_sq.append(sop(scalar, "activation", tmp[:, :],
                                     tmp[:, :], AF.Sqrt))
            if kc >= 1:
                k0 = kc - 1

                @vsec
                def _(vector, k0=k0):
                    tmp = [tA, tB][k0 % 2]
                    out = [tC, bc2][k0 % 2]
                    vector.wait_ge(cS.s, p1_sq[k0])
                    vop(vector, "tensor_scalar", tmp[:, 0:1], tmp[:, 0:1],
                        rmv_sb[:, 1:2], rmv_sb[:, 0:1], ALU.mult, ALU.add)
                    vop(vector, "tensor_tensor", xtb[:, k0, :], tmp[:, :],
                        xtb[:, k0, :], ALU.mult)
                    vop(vector, "tensor_tensor_scan", out[:, :],
                        abuf[:, k0, :], xtb[:, k0, :], 0.0, ALU.mult, ALU.add)
                    p1_xt.append(vop(vector, "tensor_copy",
                                     hcore[:, k0:k0 + 1], out[:, T - 1:T]))
        v_p1 = cV.v

        if _PHASES < 2:
            @ysec
            def _(sync):
                sync.wait_ge(cV.s, v_p1)
                sync.wait_ge(dTS2[0].s, dTS2[0].v)
                sync.wait_ge(dTS2[1].s, dTS2[1].v)
                dma(sync, dF, yT[:, :], xT[:, :])
                sync.wait_ge(dF.s, dF.v)
            return nc

        @gsec
        def _(gpsimd):
            gpsimd.wait_ge(cS.s, s_ac)
            gpsimd.wait_ge(cV.s, v_p1)
            dma(gpsimd, dCI, cc1_in[0, 0:2048].rearrange("(p kc) -> p kc",
                                                         kc=KC), acore[:, :])
            dma(gpsimd, dCI, cc1_in[0, 2048:4096].rearrange("(p kc) -> p kc",
                                                            kc=KC), hcore[:, :])
            dma(gpsimd, dCI, cc1_in[0:1, 4096:4160], eye_sb[0:1, 0:64])
            gpsimd.wait_ge(dCI.s, dCI.v)
            gpsimd.collective_compute(
                "AllGather", ALU.bypass, replica_groups=[list(range(NCOR))],
                ins=[cc1_in[:, :]], outs=[cc1_out[:, :]]).then_inc(ccc.s, 1)
            ccc.v += 1
        t_cc1 = ccc.v

        @ysec
        def _(sync):
            sync.wait_ge(ccc.s, t_cc1)
            for k in range(NCOR):
                dma(sync, dU2, exA[:, k, :],
                    cc1_out[k, 0:2048].rearrange("(p kc) -> p kc", kc=KC))
                dma(sync, dU2, exH[:, k, :],
                    cc1_out[k, 2048:4096].rearrange("(p kc) -> p kc", kc=KC))
        t_ccout = dU2.v

        @vsec
        def _(vector):
            vector.wait_ge(dU2.s, t_ccout)
            vop(vector, "tensor_copy", exg[:, :], exH[:, 0, :])
            vop(vector, "tensor_scalar", hin[:, :], exg[:, :], mkv_sb[:, 1:2],
                None, ALU.mult, ALU.bypass)
            for k in range(2, NCOR):
                vop(vector, "tensor_tensor", exg[:, :], exg[:, :],
                    exA[:, k - 1, :], ALU.mult)
                vop(vector, "tensor_tensor", exg[:, :], exg[:, :],
                    exH[:, k - 1, :], ALU.add)
                vop(vector, "scalar_tensor_tensor", hin[:, :], exg[:, :],
                    mkv_sb[:, k:k + 1], hin[:, :], ALU.mult, ALU.add)
        v_hin = cV.v


        if _PHASES < 3:
            @ysec
            def _(sync):
                sync.wait_ge(ccc.s, ccc.v)
                sync.wait_ge(cV.s, v_hin)
                dma(sync, dF, yT[:, :], xT[:, :])
                sync.wait_ge(dF.s, dF.v)
            return nc

        p3_d = []
        p3_y = []
        p3_pair = []
        for kc in range(KC):
            xs = [tC, bc0, bc1, bc2][kc % 4]

            @ysec
            def _(sync, kc=kc, xs=xs):
                if kc in (1, 2):
                    sync.wait_ge(cV.s, v_w1)
                elif kc < 4:
                    sync.wait_ge(cV.s, v_p1)
                else:
                    sync.wait_ge(cV.s, p3_y[kc - 4])
                dma(sync, dX4[kc % 4], xs[:, :], xT_c[kc])
            p3_d.append(dX4[kc % 4].v)

            @vsec
            def _(vector, kc=kc, xs=xs):
                tmp = [tA, tB][kc % 2]
                vector.wait_ge(dX4[kc % 4].s, p3_d[kc])
                if kc == 0:
                    vector.wait_ge(cV.s, v_hin)
                vop(vector, "tensor_tensor_scan", tmp[:, :], abuf[:, kc, :],
                    xtb[:, kc, :], hin[:, kc:kc + 1], ALU.mult, ALU.add)
                p3_y.append(vop(vector, "tensor_tensor", xmm[:, kc, :],
                                tmp[:, :], xs[:, :], ALU.add))
                vop(vector, "tensor_tensor", tmp[:, :], xmm[:, kc, :],
                    xmm[:, kc, :], ALU.mult)
                if kc == 0:
                    vop(vector, "tensor_copy", PC[:, :], xmm[:, kc, :])
                    vop(vector, "tensor_copy", PD[:, :], tmp[:, :])
                else:
                    vop(vector, "tensor_tensor", PC[:, :], PC[:, :],
                        xmm[:, kc, :], ALU.add)
                    vop(vector, "tensor_tensor", PD[:, :], PD[:, :], tmp[:, :],
                        ALU.add)

            if kc % 2 == 1:
                j = kc // 2

                @gsec
                def _(gpsimd, kc=kc, j=j):
                    gpsimd.wait_ge(cV.s, p3_y[kc])
                    if j >= 2:
                        gpsimd.wait_ge(dOs[j % 2].s, p3_pair[j - 2])
                    dma(gpsimd, dOs[j % 2], yb0p[j],
                        xmm[:, kc - 1:kc + 1, :].rearrange("p a b -> p (a b)"))
                p3_pair.append(dOs[j % 2].v)

        t_p3 = cT.v
        v_p3s = cV.v
        s_p3 = cS.v
        p3_ready = [(dOs[(o // 2) % 2].s, p3_pair[o // 2]) for o in range(OC)]

        for K in range(4):
            @gsec
            def _(gpsimd, K=K):
                gpsimd.wait_ge(dTS2[0].s, agv["wA%d" % K][0])
                gpsimd.wait_ge(dTS2[1].s, agv["wA%d" % K][1])
                gpsimd.collective_compute(
                    "AllGather", ALU.bypass,
                    replica_groups=[list(range(NCOR))],
                    ins=[tsA[K][:, :]], outs=[talA[K][:, :]]).then_inc(
                        ccc.s, 1)
                ccc.v += 1
            agv["a%d" % K] = ccc.v

        @gsec
        def _(gpsimd):
            gpsimd.wait_ge(dTS2[0].s, agv["fin"][0])
            gpsimd.wait_ge(dTS2[1].s, agv["fin"][1])
            gpsimd.collective_compute(
                "AllGather", ALU.bypass, replica_groups=[list(range(NCOR))],
                ins=[tsliceB[:, :]], outs=[tallB[:, :]]).then_inc(ccc.s, 1)
            ccc.v += 1
        agv["b"] = ccc.v

        def emit_ffn(src_sb, srcp, dstp, final, first, glue_waits, src_ready,
                     sub_ready):
            g = emit_ln_glue(waits=glue_waits)

            for kc in range(KC):
                @vsec
                def _(vector, kc=kc):
                    if kc == 0:
                        vector.wait_ge(cV.s, g["v2"])
                    if sub_ready[kc] is not None:
                        vector.wait_ge(*sub_ready[kc])
                    vop(vector, "tensor_tensor", xmm[:, kc, :],
                        src_sb[:, kc, :], bc0[:, :], ALU.subtract)
                    if kc == 0:
                        vop(vector, "tensor_copy", mxa[:, :], xmm[:, 0, :])
                        vop(vector, "tensor_copy", mna[:, :], xmm[:, 0, :])
                    else:
                        vop(vector, "tensor_tensor", mxa[:, :], mxa[:, :],
                            xmm[:, kc, :], ALU.max)
                        vop(vector, "tensor_tensor", mna[:, :], mna[:, :],
                            xmm[:, kc, :], ALU.min)

            @vsec
            def _(vector):
                vop(vector, "tensor_scalar", tA[:, :], mna[:, :], -1.0, None,
                    ALU.mult, ALU.bypass)
                vop(vector, "tensor_tensor", tA[:, :], tA[:, :], mxa[:, :],
                    ALU.max)
            v3 = cV.v

            @tsec
            def _(tensor):
                tensor.wait_ge(cV.s, v3)
                for j in range(8):
                    ttr(tensor, PC[:, j * 128:(j + 1) * 128],
                        tA[:, j * 128:(j + 1) * 128], eye_sb[:, :])
            t1 = cT.v

            @vsec
            def _(vector):
                vector.wait_ge(cT.s, t1)
                for j in range(8):
                    vop(vector, "reduce_max", tB[:, j:j + 1],
                        PC[:, j * 128:(j + 1) * 128], AX.X)
            v4 = cV.v

            @tsec
            def _(tensor):
                tensor.wait_ge(cV.s, v4)
                ttr(tensor, PD[:, 0:128], tB[:, 0:128], eye_sb[:, :])
            t2 = cT.v

            @vsec
            def _(vector):
                vector.wait_ge(cT.s, t2)
                vop(vector, "tensor_copy", tB[0:8, 0:128], PD[0:8, 0:128])
            v5 = cV.v

            @tsec
            def _(tensor):
                tensor.wait_ge(cV.s, v5)
                tmm(tensor, [(PC[:, j * 128:(j + 1) * 128],
                              sel8_sb[:, j * 128:(j + 1) * 128],
                              tB[0:8, 0:128], True, True)
                             for j in range(8)])
            t3 = cT.v

            @vsec
            def _(vector):
                vector.wait_ge(cT.s, t3)
                vop(vector, "tensor_tensor", tA[:, :], bc1[:, :], PC[:, :],
                    ALU.mult)
                vop(vector, "tensor_scalar", tA[:, :], tA[:, :], 1e-5, None,
                    ALU.max, ALU.bypass)
                vop(vector, "reciprocal", tB[:, :], tA[:, :])
                vop(vector, "tensor_tensor", tB[:, :], tB[:, :], bc1[:, :],
                    ALU.mult)
                vop(vector, "tensor_scalar", bc2[:, :], tB[:, :], 127.0, None,
                    ALU.mult, ALU.bypass)
                vop(vector, "tensor_scalar", bc0[:, :], tA[:, :],
                    gam6[:, 3:4], None, ALU.mult, ALU.bypass)
                vop(vector, "tensor_scalar", bc1[:, :], tA[:, :],
                    gam6[:, 4:5], None, ALU.mult, ALU.bypass)
            v6 = cV.v

            @vsec
            def _(vector):
                vector.wait_ge(cV.s, v6)
                for c_ in dWs:
                    vector.wait_ge(c_.s, c_.v)
                for kc in range(KC):
                    vop(vector, "tensor_tensor", tA[:, :], xmm[:, kc, :],
                        bc2[:, :], ALU.mult)
                    vop(vector, "tensor_scalar", actq[:, kc, :], tA[:, :],
                        MAGIC, -MAGIC, ALU.add, ALU.add)
            v_rq = cV.v

            hd_d = []
            hd_t = []
            hd_tg = []
            hd_hm = []
            hd_sl = []
            hd_mn = []
            t_hid_base = cT.v
            m_order = [8 * r + 2 * K + dm
                       for K in range(4) for r in range(MS) for dm in (0, 1)]
            for j, m in enumerate(m_order):
                r, mloc = divmod(m, MS)

                @ysec
                def _(sync, j=j, m=m, r=r, mloc=mloc):
                    if j == 0:
                        if first:
                            sync.wait_ge(dTS2[0].s, agv["fin"][0])
                            sync.wait_ge(dTS2[1].s, agv["fin"][1])
                            sync.wait_ge(cV.s, v_p3s)
                        else:
                            sync.wait_ge(cT.s, t_hid_base)
                    if first and j % 16 == 0:
                        sync.wait_ge(ccc.s, agv["a%d" % (mloc // 2)])
                    caps = []
                    for hf in (0, 1):
                        i = 2 * j + hf
                        if i >= 3:
                            sync.wait_ge(cT.s, hd_t[(i - 3) // 2])
                        base = ((mloc % 2) * 2 + hf) * PIECE
                        dma(sync, dWs[i % 3],
                            wrot[i % 3][:, :, :].rearrange("p a b -> p (a b)"),
                            talA[mloc // 2][r, base:base + PIECE].rearrange(
                                "(p q) -> p q", p=128))
                        caps.append((dWs[i % 3].s, dWs[i % 3].v))
                    hd_d.append(tuple(caps))

                @tsec
                def _(tensor, j=j, m=m):
                    PG = PGV[(2 * j) % 4]
                    PV = PGV[(2 * j + 1) % 4]
                    tensor.wait_ge(*hd_d[j][0])
                    tensor.wait_ge(*hd_d[j][1])
                    if j == 0:
                        tensor.wait_ge(cV.s, v_rq)
                    if j >= 2:
                        tensor.wait_ge(cV.s, hd_hm[j - 2])
                    wg = wrot[(2 * j) % 3]
                    wv = wrot[(2 * j + 1) % 3]
                    g_ = []
                    for kc in range(KC):
                        for hh in range(2):
                            g_.append((PG[:, hh * HL:(hh + 1) * HL],
                                       wg[:, kc, :],
                                       actq[:, kc, hh * HL:(hh + 1) * HL],
                                       kc == 0, kc == KC - 1))
                    for kc in range(KC):
                        for hh in range(2):
                            g_.append((PV[:, hh * HL:(hh + 1) * HL],
                                       wv[:, kc, :],
                                       actq[:, kc, hh * HL:(hh + 1) * HL],
                                       kc == 0, kc == KC - 1))
                    tmm(tensor, g_)
                hd_t.append(cT.v)

                @vsec
                def _(vector, j=j):
                    PG = PGV[(2 * j) % 4]
                    vector.wait_ge(cT.s, hd_t[j])
                    if j >= 1:
                        vector.wait_ge(cS.s, hd_sl[j - 1])
                    hd_tg.append(vop(vector, "tensor_tensor", tA[:, :],
                                     PG[:, :], bc0[:, :], ALU.mult))

                @ssec
                def _(scalar, j=j, m=m):
                    scalar.wait_ge(cV.s, hd_tg[j])
                    sop(scalar, "activation", tC[:, :], tA[:, :], AF.Silu,
                        bias=gb_sb[:, m:m + 1])
                hd_sl.append(cS.v)

                @vsec
                def _(vector, j=j, m=m):
                    PV = PGV[(2 * j + 1) % 4]
                    vector.wait_ge(cS.s, hd_sl[j])
                    vop(vector, "tensor_tensor", tB[:, :], PV[:, :], bc1[:, :],
                        ALU.mult)
                    vop(vector, "scalar_tensor_tensor", hmid[:, m, :], tB[:, :],
                        vb_sb[:, m:m + 1], tC[:, :], ALU.add, ALU.mult)
                    if j == 0:
                        vop(vector, "tensor_copy", mxa[:, :], hmid[:, m, :])
                        vop(vector, "tensor_copy", mna[:, :], hmid[:, m, :])
                    else:
                        vop(vector, "tensor_tensor", mxa[:, :], mxa[:, :],
                            hmid[:, m, :], ALU.max)
                        vop(vector, "tensor_tensor", mna[:, :], mna[:, :],
                            hmid[:, m, :], ALU.min)
                hd_hm.append(cV.v)
            t_hid = cT.v
            v_hid = cV.v

            @vsec
            def _(vector):
                vop(vector, "tensor_scalar", tA[:, :], mna[:, :], -1.0, None,
                    ALU.mult, ALU.bypass)
                vop(vector, "tensor_tensor", tA[:, :], tA[:, :], mxa[:, :],
                    ALU.max)
            v7 = cV.v

            @tsec
            def _(tensor):
                tensor.wait_ge(cV.s, v7)
                for j in range(8):
                    ttr(tensor, PA[:, j * 128:(j + 1) * 128],
                        tA[:, j * 128:(j + 1) * 128], eye_sb[:, :])
            t4 = cT.v

            @vsec
            def _(vector):
                vector.wait_ge(cT.s, t4)
                for j in range(8):
                    vop(vector, "reduce_max", tB[:, j:j + 1],
                        PA[:, j * 128:(j + 1) * 128], AX.X)
            v8 = cV.v

            @tsec
            def _(tensor):
                tensor.wait_ge(cV.s, v8)
                ttr(tensor, PB[:, 0:128], tB[:, 0:128], eye_sb[:, :])
            t5 = cT.v

            @vsec
            def _(vector):
                vector.wait_ge(cT.s, t5)
                vop(vector, "tensor_copy", tB[0:8, 0:128], PB[0:8, 0:128])
            v9 = cV.v

            @tsec
            def _(tensor):
                tensor.wait_ge(cV.s, v9)
                tmm(tensor, [(PA[:, j * 128:(j + 1) * 128],
                              sel8_sb[:, j * 128:(j + 1) * 128],
                              tB[0:8, 0:128], True, True)
                             for j in range(8)])
            t6 = cT.v

            @vsec
            def _(vector):
                vector.wait_ge(cT.s, t6)
                vop(vector, "tensor_scalar", tA[:, :], PA[:, :], 1e-5, None,
                    ALU.max, ALU.bypass)
                vop(vector, "reciprocal", bc2[:, :], tA[:, :])
                vop(vector, "tensor_scalar", bc2[:, :], bc2[:, :], 127.0, None,
                    ALU.mult, ALU.bypass)
                vop(vector, "tensor_scalar", bc0[:, :], tA[:, :],
                    gam6[:, 5:6], None, ALU.mult, ALU.bypass)
            v10 = cV.v

            q3 = []

            @vsec
            def _(vector):
                vector.wait_ge(cV.s, v10)
                for m in range(HC):
                    vop(vector, "tensor_tensor", tA[:, :], hmid[:, m, :],
                        bc2[:, :], ALU.mult)
                    q3.append(vop(vector, "tensor_scalar", hmid[:, m, :],
                                  tA[:, :], MAGIC, -MAGIC, ALU.add, ALU.add))
            v_q3 = cV.v

            dOut = dF if final else dOQ
            dst_ready = []
            op_q = []
            op_d = []
            op_rd = []
            op_t = []
            op_tg = []
            op_y = []
            op_bias = []
            op_sq = []
            op_o = []
            op_ln = []
            for o in range(OC):
                r, oloc = divmod(o, 2)

                @ysec
                def _(sync, o=o, r=r, oloc=oloc):
                    caps = []
                    for qq in range(4):
                        i = 4 * o + qq
                        if i < 6:
                            sync.wait_ge(cT.s, t_hid)
                            if first and i == 0:
                                sync.wait_ge(ccc.s, agv["b"])
                        else:
                            sync.wait_ge(cT.s, op_q[i - 6])
                        base = oloc * PIECE * 4 + qq * PIECE
                        ct6 = [dWs[0], dWs[1], dWs[2], dQx[0], dQx[1],
                               dX4[3]][i % 6]
                        dma(sync, ct6,
                            wrot6[i % 6][:, :, :].rearrange(
                                "p a b -> p (a b)"),
                            tallB[r, base:base + PIECE].rearrange(
                                "(p q) -> p q", p=128))
                        caps.append((ct6.s, ct6.v))
                    op_d.append(caps)
                    if o < 2:
                        sync.wait_ge(cV.s, v_hid)
                    else:
                        sync.wait_ge(cV.s, op_y[o - 2])
                    sync.wait_ge(*src_ready[o])
                    dma(sync, dXs[o % 2], [tC, bc1][o % 2][:, :], half(srcp, o))
                    op_rd.append(dXs[o % 2].v)

                @tsec
                def _(tensor, o=o):
                    PO = [PA, PB][o % 2]
                    if o >= 2:
                        tensor.wait_ge(cV.s, op_tg[o - 2])
                    for qq in range(4):
                        tensor.wait_ge(*op_d[o][qq])
                        if o == 0:
                            tensor.wait_ge(cV.s, q3[16 * (qq + 1) - 1])
                        g_ = []
                        for kk in range(16):
                            kcg = qq * 16 + kk
                            wt = wrot6[(4 * o + qq) % 6]
                            for hh in range(2):
                                g_.append((PO[:, hh * HL:(hh + 1) * HL],
                                           wt[:, kk, :],
                                           hmid[:, kcg,
                                                hh * HL:(hh + 1) * HL],
                                           kcg == 0, kcg == HC - 1))
                        tmm(tensor, g_)
                        op_q.append(cT.v)
                op_t.append(cT.v)

                @vsec
                def _(vector, o=o):
                    PO = [PA, PB][o % 2]
                    vector.wait_ge(cT.s, op_t[o])
                    op_tg.append(vop(vector, "tensor_tensor", tA[:, :],
                                     PO[:, :], bc0[:, :], ALU.mult))

                @vsec
                def _(vector, o=o):
                    vector.wait_ge(dXs[o % 2].s, op_rd[o])
                    if o >= 1:
                        vector.wait_ge(dOut.s, op_o[o - 1])
                    op_y.append(vop(vector, "scalar_tensor_tensor", tB[:, :],
                                    tA[:, :], ob_sb[:, o:o + 1],
                                    [tC, bc1][o % 2][:, :], ALU.add, ALU.add))
                    if not final:
                        vop(vector, "tensor_tensor", tA[:, :], tB[:, :],
                            tB[:, :], ALU.mult)
                        if o == 0:
                            vop(vector, "tensor_copy", PC[:, :], tB[:, :])
                            vop(vector, "tensor_copy", PD[:, :], tA[:, :])
                        else:
                            vop(vector, "tensor_tensor", PC[:, :], PC[:, :],
                                tB[:, :], ALU.add)
                            vop(vector, "tensor_tensor", PD[:, :], PD[:, :],
                                tA[:, :], ALU.add)

                @gsec
                def _(gpsimd, o=o):
                    gpsimd.wait_ge(cV.s, op_y[o])
                    dma(gpsimd, dOut,
                        yT_c[o] if final else half(dstp, o), tB[:, :])
                op_o.append(dOut.v)
                dst_ready.append((dOut.s, dOut.v))

            return {"cT": cT.v, "cS": cS.v, "cV": cV.v,
                    "dst_ready": dst_ready}

        def emit_attnb(srcp, dstp, glue_waits, src_ready):
            g = emit_ln_glue(waits=glue_waits)
            t_enter = cT.v
            lands = [tC, bc2, tB]
            lctr = [dXs[0], dXs[1], dX4[2]]

            @gsec
            def _(gpsimd):
                gpsimd.wait_ge(cT.s, t_enter)
                dma(gpsimd, dGu, wbr_b[:, :, :],
                    wbr.rearrange("(kc p) r -> p kc r", p=128))
                dma(gpsimd, dGm, wcm_b[:, :], wcm[:, :])
            t_wb = (dGu.v, dGm.v)

            u_d = []
            u_v = []
            for kc in range(KC):
                @ysec
                def _(sync, kc=kc):
                    if kc < 3:
                        sync.wait_ge(cV.s, g["v2"])
                    else:
                        sync.wait_ge(cV.s, u_v[kc - 3])
                    sync.wait_ge(*src_ready[kc])
                    dma(sync, lctr[kc % 3], lands[kc % 3][:, :],
                        half(srcp, kc))
                u_d.append(lctr[kc % 3].v)

                @vsec
                def _(vector, kc=kc):
                    vector.wait_ge(lctr[kc % 3].s, u_d[kc])
                    if kc == 0:
                        vector.wait_ge(cV.s, g["v2"])
                        for c_ in dWs:
                            vector.wait_ge(c_.s, c_.v)
                    vop(vector, "tensor_tensor", tA[:, :],
                        lands[kc % 3][:, :], bc0[:, :], ALU.subtract)
                    u_v.append(vop(vector, "tensor_tensor", actq[:, kc, :],
                                   tA[:, :], bc1[:, :], ALU.mult))
            v_u = cV.v

            @tsec
            def _(tensor):
                tensor.wait_ge(cV.s, v_u)
                tensor.wait_ge(dGu.s, t_wb[0])
                tensor.wait_ge(dGm.s, t_wb[1])
                g_ = []
                for kc in range(KC):
                    for hh in range(2):
                        g_.append((PA[0:64, hh * HL:(hh + 1) * HL],
                                   wbr_b[:, kc, :],
                                   actq[:, kc, hh * HL:(hh + 1) * HL],
                                   kc == 0, kc == KC - 1))
                tmm(tensor, g_)
            t_t1 = cT.v

            @vsec
            def _(vector):
                vector.wait_ge(cT.s, t_t1)
                vop(vector, "tensor_copy", t1b[:, :], PA[0:64, :])
            v_t1 = cV.v

            @tsec
            def _(tensor):
                tensor.wait_ge(cV.s, v_t1)
                tmm(tensor, halves(PB[0:64], wmid_b[:, :],
                                   lambda hh: t1b[:, hh * HL:(hh + 1) * HL],
                                   True, True))
            t_t2 = cT.v

            @vsec
            def _(vector):
                vector.wait_ge(cT.s, t_t2)
                vop(vector, "tensor_copy", t2b[:, :], PB[0:64, :])
            v_t2 = cV.v

            r_rd = []
            r_t = []
            r_du = []
            r_y = []
            r_pair = []
            for o in range(OC):
                @ysec
                def _(sync, o=o):
                    if o < 3:
                        sync.wait_ge(cV.s, v_u)
                    else:
                        sync.wait_ge(cV.s, r_y[o - 3])
                    sync.wait_ge(*src_ready[o])
                    dma(sync, lctr[o % 3], lands[o % 3][:, :], half(srcp, o))
                r_rd.append(lctr[o % 3].v)

                @tsec
                def _(tensor, o=o):
                    PO = [PA, PB][o % 2]
                    if o == 0:
                        tensor.wait_ge(cV.s, v_t2)
                    if o >= 2:
                        tensor.wait_ge(cV.s, r_du[o - 2])
                    tmm(tensor, halves(PO, wcm_b[:, o * 128:(o + 1) * 128],
                                       lambda hh: t2b[:, hh * HL:(hh + 1) * HL],
                                       True, True))
                r_t.append(cT.v)

                @vsec
                def _(vector, o=o):
                    PO = [PA, PB][o % 2]
                    vector.wait_ge(cT.s, r_t[o])
                    vop(vector, "tensor_scalar", tA[:, :], actq[:, o, :],
                        dv_sb[:, o:o + 1], None, ALU.mult, ALU.bypass)
                    r_du.append(vop(vector, "tensor_tensor", tA[:, :], tA[:, :],
                                    PO[:, :], ALU.add))
                    vector.wait_ge(lctr[o % 3].s, r_rd[o])
                    r_y.append(vop(vector, "tensor_tensor", ymm[:, o, :],
                                   tA[:, :], lands[o % 3][:, :], ALU.add))
                    vop(vector, "tensor_tensor", tA[:, :], ymm[:, o, :],
                        ymm[:, o, :], ALU.mult)
                    if o == 0:
                        vop(vector, "tensor_copy", PC[:, :], ymm[:, o, :])
                        vop(vector, "tensor_copy", PD[:, :], tA[:, :])
                    else:
                        vop(vector, "tensor_tensor", PC[:, :], PC[:, :],
                            ymm[:, o, :], ALU.add)
                        vop(vector, "tensor_tensor", PD[:, :], PD[:, :],
                            tA[:, :], ALU.add)

                if o % 2 == 1:
                    j = o // 2

                    @gsec
                    def _(gpsimd, o=o, j=j):
                        gpsimd.wait_ge(cV.s, r_y[o])
                        if j >= 1:
                            gpsimd.wait_ge(dOQ.s, r_pair[j - 1])
                        dma(gpsimd, dOQ, dstp[j],
                            ymm[:, o - 1:o + 1, :].rearrange(
                                "p a b -> p (a b)"))
                    r_pair.append(dOQ.v)

            return {"cT": cT.v, "cS": cS.v, "cV": cV.v,
                    "dst_ready": [(dOQ.s, r_pair[o // 2]) for o in range(OC)]}

        ffn1 = ab = None
        if _SUBLAYERS >= 2 and _PHASES >= 4:
            ffn1 = emit_ffn(xmm, yb0p, yb1p,
                            final=(_SUBLAYERS == 2), first=True,
                            glue_waits=[(cV.s, v_p3s), (cV.s, cV.v)],
                            src_ready=p3_ready,
                            sub_ready=p3_ready)
        if _SUBLAYERS >= 3 and _PHASES >= 5 and ffn1 is not None:
            ab = emit_attnb(yb1p, yb0p,
                            glue_waits=[(cT.s, ffn1["cT"]), (cS.s, ffn1["cS"]),
                                        (cV.s, cV.v), ffn1["dst_ready"][-1]],
                            src_ready=ffn1["dst_ready"])
        if _SUBLAYERS >= 4 and _PHASES >= 6 and ab is not None:
            ab_sub = [None] + ab["dst_ready"][:KC - 1]
            emit_ffn(ymm, yb0p, None, final=True, first=False,
                     glue_waits=[(cT.s, ab["cT"]), (cS.s, ab["cS"]),
                                 (cV.s, cV.v)],
                     src_ready=ab["dst_ready"],
                     sub_ready=ab_sub)

        done_via_ffn = (_SUBLAYERS >= 4 and _PHASES >= 6 and ab is not None) or (
            _SUBLAYERS == 2 and _PHASES >= 4)
        if not done_via_ffn:
            @ysec
            def _(sync):
                sync.wait_ge(dOs[0].s, dOs[0].v)
                sync.wait_ge(dOs[1].s, dOs[1].v)
                sync.wait_ge(dOQ.s, dOQ.v)
                sync.wait_ge(ccc.s, ccc.v)
                use_yb1 = (_SUBLAYERS == 2 and _PHASES >= 4) or (
                    _SUBLAYERS > 2 and _PHASES == 4)
                src = yb1p if use_yb1 else yb0p
                yT_jc = yT.rearrange("(j c p) t -> c j p t", c=2, p=128)
                dma(sync, dF, yT_jc[0], src[:, :, 0:T])
                dma(sync, dF, yT_jc[1], src[:, :, T:2 * T])

        @ysec
        def _(sync):
            sync.wait_ge(dF.s, dF.v)

    return nc


def _host_inputs(inputs):
    x = np.asarray(inputs["L0_attn_a_x"])[0]
    igw = np.ascontiguousarray(np.asarray(inputs["L0_attn_a_ig_w"],
                                          dtype=np.float32))
    agw = np.ascontiguousarray(np.asarray(inputs["L0_attn_a_ag_w"],
                                          dtype=np.float32))
    igb = np.ascontiguousarray(np.asarray(inputs["L0_attn_a_ig_b"],
                                          dtype=np.float32).T)
    agb = np.ascontiguousarray(np.asarray(inputs["L0_attn_a_ag_b"],
                                          dtype=np.float32).T)
    apar = np.ascontiguousarray(
        np.asarray(inputs["L0_attn_a_a_param"], dtype=np.float32
                   ).reshape(KC, 128).T)
    dvec = np.ascontiguousarray(
        np.asarray(inputs["L0_attn_b_D_vec"], dtype=np.float32
                   ).reshape(KC, 128).T)
    gbias = np.ascontiguousarray(
        np.asarray(inputs["L0_ffn_gate_b"], dtype=np.float32).reshape(HC, 128).T)
    vbias = np.ascontiguousarray(
        np.asarray(inputs["L0_ffn_val_b"], dtype=np.float32).reshape(HC, 128).T)
    obias = np.ascontiguousarray(
        np.asarray(inputs["L0_ffn_out_b"], dtype=np.float32).reshape(OC, 128).T)
    gate_wT = np.ascontiguousarray(np.asarray(inputs["L0_ffn_gate_w"],
                                              dtype=np.float32).T)
    val_wT = np.ascontiguousarray(np.asarray(inputs["L0_ffn_val_w"],
                                             dtype=np.float32).T)
    out_wT = np.ascontiguousarray(np.asarray(inputs["L0_ffn_out_w"],
                                             dtype=np.float32).T)
    wbr = np.ascontiguousarray(np.asarray(inputs["L0_attn_b_W_b_rank"],
                                          dtype=np.float32))
    wbsT = np.ascontiguousarray(
        np.asarray(inputs["L0_attn_b_W_b_state"],
                   dtype=np.float32).T.reshape(2, 128, 64))
    wcr = np.ascontiguousarray(
        np.asarray(inputs["L0_attn_b_W_c_rank"],
                   dtype=np.float32).reshape(2, 128, 64))
    wcm = np.ascontiguousarray(np.asarray(inputs["L0_attn_b_W_c_model"],
                                          dtype=np.float32))
    A = np.asarray(inputs["L0_attn_b_A_factors"], dtype=np.float32)

    idx = np.arange(256)
    bit = [((idx >> (7 - l)) & 1) for l in range(8)]
    msk_l = [((idx & ~(1 << (7 - l)))[:, None] ==
              (idx & ~(1 << (7 - l)))[None, :]).astype(np.float32)
             for l in range(8)]
    kgv = np.empty((8, 256, 256), np.float32)
    kmv = np.empty((8, 256, 256), np.float32)
    kgv[0] = A[7][bit[7][:, None], bit[7][None, :]]
    kmv[0] = msk_l[7]
    for i, l in enumerate(range(6, -1, -1)):
        kgv[1 + i] = A[l][bit[l][None, :], bit[l][:, None]]
        kmv[1 + i] = msk_l[l]

    eye = np.eye(128, dtype=np.float32)
    onescol = np.ones((128, 1), np.float32)
    ones_r = np.ones((1, 128), np.float32)
    sel8v = np.ascontiguousarray(np.repeat(np.eye(8, dtype=np.float32), 128,
                                           axis=1))

    def tile_w(wTmat, cols, c):
        sl = wTmat[:, c * cols:(c + 1) * cols]
        nk = wTmat.shape[0] // 128
        t = sl.reshape(nk, 128, cols)
        mloc = cols // 128
        t2 = t.reshape(nk, 128, mloc, 128).transpose(2, 1, 0, 3)
        return np.ascontiguousarray(t2.reshape(mloc, 128, nk * 128))

    maps = []
    for c in range(NCOR):
        xT_ = np.ascontiguousarray(x[c * T:(c + 1) * T, :].T.astype(np.float32))
        rmvv = np.zeros((128, 2), np.float32)
        rmvv[:, 0] = 1.0 if c == 0 else 0.0
        rmvv[:, 1] = 0.0 if c == 0 else 1.0
        mkvv = np.zeros((128, 8), np.float32)
        mkvv[:, c] = 1.0
        def pair_pieces(w8):
            return np.ascontiguousarray(
                w8.reshape(4, 2, 128, 2048).transpose(0, 2, 1, 3).reshape(
                    4, 128, 4096))

        wsg_ = pair_pieces(tile_w(gate_wT, 1024, c))
        wsv_ = pair_pieces(tile_w(val_wT, 1024, c))
        wso4 = tile_w(out_wT, 256, c)
        wso_ = pair_pieces(np.ascontiguousarray(
            wso4.reshape(2, 128, 4, 2048).transpose(0, 2, 1, 3).reshape(
                8, 128, 2048)))
        maps.append({
            "xT": xT_, "igw": igw, "agw": agw, "igb": igb, "agb": agb,
            "apar": apar, "dvec": dvec, "gbias": gbias, "vbias": vbias,
            "obias": obias, "wsg": wsg_, "wsv": wsv_, "wso": wso_,
            "wbr": wbr, "wbsT": wbsT, "wcr": wcr, "wcm": wcm,
            "kg": np.ascontiguousarray(kgv.reshape(8, 2, 128, 256)),
            "km": np.ascontiguousarray(kmv.reshape(8, 2, 128, 256)),
            "eye": eye, "onescol": onescol, "ones_r": ones_r,
            "rmv": rmvv, "mkv": mkvv, "sel8": sel8v,
        })
    return maps


def kernel(**inputs):
    maps = _host_inputs(inputs)
    nc = build_nc()
    res = run_bass_kernel_spmd(nc, maps, core_ids=list(range(NCOR)))
    out = np.empty((1, NCOR * T, D), np.float32)
    for c in range(NCOR):
        out[0, c * T:(c + 1) * T, :] = res.results[c]["yT"].T
    return out
